# version 65 (final)
# baseline (speedup 1.0000x reference)
import numpy as np
import ml_dtypes
import concourse.bass as bass
import concourse.mybir as mybir
from concourse.bass_utils import run_bass_kernel_spmd
from concourse import dve_ops

F32 = mybir.dt.float32
BF16 = mybir.dt.bfloat16
FP8 = mybir.dt.float8e4
AF = mybir.ActivationFunctionType
ALU = mybir.AluOpType
AX = mybir.AxisListType

S = 2048
DM = 1024
NPAIR = 8
FF = 4096
NT = 16
ND = 8
NQ = 4
NF = 32
SCALE = 0.125
EPS = 1e-5

ENGS = ("tensor", "scalar", "vector", "gpsimd", "sync")


class Reg:
    __slots__ = ("w", "rs")

    def __init__(self, deps=()):
        self.w = None
        self.rs = list(deps)

    def tokens(self):
        t = list(self.rs)
        if self.w:
            t.append(self.w)
        return t


class Sch:
    SEM_OF = {"tensor": "T", "scalar": "S", "vector": "V", "gpsimd": "G"}

    def __init__(self):
        self.ops = {e: [] for e in ENGS}
        self.n = {s: 0 for s in ("T", "S", "V", "G")}
        self.cw = {}

    def is_dma_sem(self, s):
        return s.startswith("D")

    def _emit(self, eng, fn, reads, writes, sem, amt):
        need = {}
        for r in reads:
            if r.w is not None:
                s, v = r.w
                if need.get(s, 0) < v:
                    need[s] = v
        for wr in writes:
            for s, v in wr.tokens():
                if need.get(s, 0) < v:
                    need[s] = v
        waits = []
        for s, v in need.items():
            if eng == "tensor" and s == "T":
                continue
            if self.cw.get((eng, s), 0) >= v:
                continue
            waits.append((s, v))
            self.cw[(eng, s)] = v
        self.n[sem] += amt
        tok = (sem, self.n[sem])
        self.ops[eng].append((waits, fn, sem, amt, self.n[sem]))
        for r in reads:
            r.rs.append(tok)
        for wr in writes:
            wr.w = tok
            wr.rs = []
        return tok

    def op(self, eng, fn, reads=(), writes=()):
        return self._emit(eng, fn, reads, writes, self.SEM_OF[eng], 1)

    def dma(self, eng, fn, dsem, reads=(), writes=()):
        self.n.setdefault(dsem, 0)
        return self._emit(eng, fn, reads, writes, dsem, 16)


def build_program():
    nc = bass.Bass("TRN2", target_bir_lowering=False)
    sch = Sch()
    T, SC, V, G = "tensor", "scalar", "vector", "gpsimd"

    d_x = nc.dram_tensor("x_nat", [S, DM], F32, kind="ExternalInput")
    d_xT = nc.dram_tensor("xT", [DM, S], F32, kind="ExternalInput")
    d_wq = nc.dram_tensor("Wq", [DM, DM], F32, kind="ExternalInput")
    d_wk = nc.dram_tensor("Wk", [DM, DM], F32, kind="ExternalInput")
    d_wv = nc.dram_tensor("Wv", [DM, DM], F32, kind="ExternalInput")
    d_wo = nc.dram_tensor("Wo", [DM, DM], F32, kind="ExternalInput")
    d_w1t = nc.dram_tensor("W1T", [DM, FF], F32, kind="ExternalInput")
    d_w2t = nc.dram_tensor("W2T", [FF, DM], F32, kind="ExternalInput")
    d_bq = nc.dram_tensor("bq8", [8, 128], F32, kind="ExternalInput")
    d_bk = nc.dram_tensor("bk8", [8, 128], F32, kind="ExternalInput")
    d_b1 = nc.dram_tensor("b1_32", [32, 128], F32, kind="ExternalInput")
    d_bv = nc.dram_tensor("bv_row", [1, DM], F32, kind="ExternalInput")
    d_bo = nc.dram_tensor("bo_row", [1, DM], F32, kind="ExternalInput")
    d_b2 = nc.dram_tensor("b2_row", [1, DM], F32, kind="ExternalInput")
    d_rk = nc.dram_tensor("relk", [5, 64], F32, kind="ExternalInput")
    d_rv = nc.dram_tensor("relv", [5, 64], F32, kind="ExternalInput")
    d_idf = nc.dram_tensor("identf", [128, 128], F32, kind="ExternalInput")
    d_idb = nc.dram_tensor("idd129", [128, 129], BF16, kind="ExternalInput")
    d_idu = nc.dram_tensor("identu129", [128, 129], BF16, kind="ExternalInput")
    d_tng = nc.dram_tensor("m0c129", [128, 129], BF16, kind="ExternalInput")
    d_oxl = nc.dram_tensor("onesxl", [128, 1], BF16, kind="ExternalInput")
    d_els = nc.dram_tensor("elast", [128, 1], F32, kind="ExternalInput")
    d_o1f = nc.dram_tensor("onesrow_f", [1, 128], F32, kind="ExternalInput")
    d_o1b = nc.dram_tensor("onesrow_b", [1, 128], BF16, kind="ExternalInput")
    d_oall = nc.dram_tensor("onesall", [128, 128], BF16, kind="ExternalInput")
    d_i128 = nc.dram_tensor("inv128", [128, 128], BF16, kind="ExternalInput")
    d_out = nc.dram_tensor("out", [S, DM], F32, kind="ExternalOutput")

    from contextlib import ExitStack
    with ExitStack() as CM:
        E = CM.enter_context
        identf = E(nc.sbuf_tensor("identf_sb", [128, 128], F32))
        idd129 = E(nc.sbuf_tensor("idd129_sb", [128, 129], BF16))
        identu = E(nc.sbuf_tensor("identu_sb", [128, 129], BF16))
        m0c = E(nc.sbuf_tensor("m0c_sb", [128, 129], BF16))
        onesxl = E(nc.sbuf_tensor("onesxl_sb", [128, 1], BF16))
        elast = E(nc.sbuf_tensor("elast_sb", [128, 1], F32))
        o1f = E(nc.sbuf_tensor("o1f_sb", [1, 128], F32))
        o1b = E(nc.sbuf_tensor("o1b_sb", [1, 128], BF16))
        oall = E(nc.sbuf_tensor("oall_sb", [128, 128], BF16))
        inv128 = E(nc.sbuf_tensor("inv128_sb", [128, 128], BF16))
        bq8 = E(nc.sbuf_tensor("bq8_sb", [8, 128], F32))
        bk8 = E(nc.sbuf_tensor("bk8_sb", [8, 128], F32))
        b1s = E(nc.sbuf_tensor("b1s_sb", [32, 128], F32))
        relks = E(nc.sbuf_tensor("relks_sb", [5, 128], F32))
        relvs = E(nc.sbuf_tensor("relvs_sb", [5, 128], F32))
        relkT = E(nc.sbuf_tensor("relkT_sb", [128, 5], F32))
        relvT = E(nc.sbuf_tensor("relvT_sb", [128, 5], F32))
        bqcols = E(nc.sbuf_tensor("bqcols", [128, 8], F32))
        bkpcols = E(nc.sbuf_tensor("bkpcols", [128, 8], F32))
        b1cols = E(nc.sbuf_tensor("b1cols", [128, 32], F32))
        TzE = E(nc.sbuf_tensor("TzE", [128, 2], BF16))
        TzO = E(nc.sbuf_tensor("TzO", [128, 2], BF16))
        TvBmE = E(nc.sbuf_tensor("TvBmE", [128, 2], F32))
        TvBmO = E(nc.sbuf_tensor("TvBmO", [128, 2], F32))
        TvBm34 = E(nc.sbuf_tensor("TvBm34", [128, 34], F32))
        Tv2zE = E(nc.sbuf_tensor("Tv2zE", [128, 128], BF16))
        Tv2zO = E(nc.sbuf_tensor("Tv2zO", [128, 128], BF16))
        sel1E = E(nc.sbuf_tensor("sel1E", [128, 2], BF16))
        sel1O = E(nc.sbuf_tensor("sel1O", [128, 2], BF16))
        sel2E = E(nc.sbuf_tensor("sel2E", [128, 2], BF16))
        sel2O = E(nc.sbuf_tensor("sel2O", [128, 2], BF16))
        sel41 = E(nc.sbuf_tensor("sel41", [128, 4], BF16))
        sel42 = E(nc.sbuf_tensor("sel42", [128, 4], BF16))
        Tz4 = E(nc.sbuf_tensor("Tz4", [128, 4], BF16))
        TvBm4 = E(nc.sbuf_tensor("TvBm4", [128, 4], F32))
        Tv2z4E = E(nc.sbuf_tensor("Tv2z4E", [128, 128], BF16))
        Tv2z4O = E(nc.sbuf_tensor("Tv2z4O", [128, 128], BF16))
        bvVb = E(nc.sbuf_tensor("bvVb", [128, 1536], BF16))
        bob = E(nc.sbuf_tensor("bo_bcast", [128, DM], BF16))
        b2b = E(nc.sbuf_tensor("b2_bcast", [128, DM], BF16))
        onescol = E(nc.sbuf_tensor("onescol", [128, 1], F32))
        accW1 = E(nc.sbuf_tensor("accW1", [128, 1], F32))
        accW2 = E(nc.sbuf_tensor("accW2", [128, 1], F32))
        gtmp = E(nc.sbuf_tensor("gtmp", [128, 1], F32))
        accWall = E(nc.sbuf_tensor("accWall", [128, 64], F32))
        gam = E(nc.sbuf_tensor("gam", [1, 2], F32))
        thrp = E(nc.sbuf_tensor("thrp", [1, 4], F32))
        thrc1 = E(nc.sbuf_tensor("thrc1", [128, 2], F32))
        thrc2 = E(nc.sbuf_tensor("thrc2", [128, 2], F32))
        E12 = [E(nc.sbuf_tensor(f"E12_{i}", [128, 512], BF16)) for i in range(2)]
        bnst = E(nc.sbuf_tensor("bnst", [128, 24], F32))
        bnagg = E(nc.sbuf_tensor("bnagg", [128, 4], F32))
        sdcol = E(nc.sbuf_tensor("sdcol", [128, 2], F32))
        rlncol = E(nc.sbuf_tensor("rlncol", [128, 2], F32))
        absp = E(nc.sbuf_tensor("absp", [128, 1], F32))
        epsc = E(nc.sbuf_tensor("epsc", [128, 2], F32))
        XT = E(nc.sbuf_tensor("XT", [128, 16384], BF16))
        KT = E(nc.sbuf_tensor("KT", [128, 16384], BF16))
        stage = [E(nc.sbuf_tensor(f"stage_{i}", [128, 1024], F32)) for i in range(2)]
        wbuf_cm = nc.sbuf_tensor("WBUF", [128, 8192], BF16, side="right")
        WBUF = wbuf_cm.__enter__()
        qt_cm = nc.sbuf_tensor("QT", [128, 16384], BF16)
        QT = qt_cm.__enter__()
        vh_cm = nc.sbuf_tensor("VH", [128, 24576], BF16)
        VH = vh_cm.__enter__()
        row_cm = [nc.sbuf_tensor(n, sh, F32) for n, sh in
                  (("bvrow_sb", [1, DM]), ("borow_sb", [1, DM]),
                   ("b2row_sb", [1, DM]), ("bvT0row", [1, DM]),
                   ("bvVrow", [1, 1536]))]
        bvrow, borow, b2row, bvT0row, bvVrow = [cm.__enter__() for cm in row_cm]

        PB = [E(nc.psum_tensor(f"pb{i}", [128, 512], F32)) for i in range(8)]

        sems = {s: E(nc.semaphore(f"sem_{s}"))
                for s in ("T", "S", "V", "G", "DSU", "DSV", "DVH", "DWV",
                          "DWO", "DS0", "DS1", "DX0", "DX1", "DO0", "DO1",
                          "DAB0", "DAB1", "DAB2", "DAB3",
                          "DAB4", "DAB5", "DAB6", "DAB7")}

        rPB = [Reg() for _ in range(8)]
        rQT = Reg()
        rKT = Reg()
        rStage = [Reg(), Reg()]
        rE = [Reg() for _ in range(4)]
        rE12 = [Reg(), Reg()]
        rStk = Reg()
        rRinvb = [Reg(), Reg()]
        rAB4 = [Reg(), Reg()]
        rAbsp = Reg()
        rRf = [Reg(), Reg()]
        rRb = [Reg(), Reg()]
        rSmall = Reg()
        rLate = Reg()
        rGam = Reg()
        rThr = Reg()
        rAcc1 = Reg()
        rAcc2 = Reg()
        rGtmp = Reg()
        rBnP = [Reg(), Reg()]

        def vop(fn, reads=(), writes=()):
            return sch.op(V, fn, reads, writes)

        def sop(fn, reads=(), writes=()):
            return sch.op(SC, fn, reads, writes)

        def top(fn, reads=(), writes=()):
            return sch.op(T, fn, reads, writes)

        def ld(dst, src, q="sync"):
            sch.dma(q, lambda e, d=dst, s=src: e.dma_start(d, s),
                    "DSU" if q == "scalar" else "DSV",
                    writes=[rSmall if q == "scalar" else rLate])

        ld(identf[:, :], d_idf[:, :], "scalar")
        ld(bq8[:, :], d_bq[:, :], "scalar")
        ld(bk8[:, :], d_bk[:, :], "scalar")
        ld(relks[0:5, 0:64], d_rk[:, :], "scalar")
        ld(relvs[0:5, 0:64], d_rv[:, :], "scalar")
        ld(b1s[:, :], d_b1[:, :], "scalar")

        rXTk = [Reg() for _ in range(ND)]
        rWBs = [Reg() for _ in range(ND)]
        rWkk = [Reg() for _ in range(ND)]
        for k in range(ND):
            thr = [rXTk[k - 3]] if k >= 3 else []
            sch.dma(G, lambda e, k=k: e.dma_start(
                WBUF[:, 1024 * k:1024 * k + 1024],
                d_wq[128 * k:128 * k + 128, :]), f"DAB{k}",
                reads=list(thr), writes=[rWBs[k]])
            tok = sch.dma(G, lambda e, k=k: e.dma_start(
                XT[:, 2048 * k:2048 * k + 2048],
                d_xT[128 * k:128 * k + 128, :]), f"DAB{k}",
                reads=list(thr), writes=[rXTk[k]])
            rWBs[k].w = tok
        ld(idd129[:, :], d_idb[:, :])
        ld(identu[:, :], d_idu[:, :])
        ld(m0c[:, :], d_tng[:, :])
        ld(onesxl[:, :], d_oxl[:, :])
        ld(elast[:, :], d_els[:, :])
        ld(o1f[:, :], d_o1f[:, :])
        ld(o1b[:, :], d_o1b[:, :])
        ld(oall[:, :], d_oall[:, :])
        ld(inv128[:, :], d_i128[:, :])
        ld(bvrow[:, :], d_bv[:, :])
        ld(borow[:, :], d_bo[:, :])
        ld(b2row[:, :], d_b2[:, :])

        for k in range(ND):
            tok = sch.dma(G, lambda e, k=k: e.dma_start(
                VH[:, 8192 + 1024 * k:8192 + 1024 * k + 1024],
                d_wk[128 * k:128 * k + 128, :]), "DVH",
                reads=[rXTk[7]] if k == 0 else [], writes=[rWkk[k]])
        for k in range(ND):
            rWkk[k].w = tok

        for e12, r in zip(E12, rE12):
            vop(lambda e, tt=e12: e.memset(tt[:, :], 0.0), writes=[r])
        vop(lambda e: e.memset(onescol[:, :], 1.0), writes=[rSmall])
        vop(lambda e: e.memset(PB[5][:, 0:8], 0.0), writes=[rPB[5]])
        vop(lambda e: e.memset(PB[4][0:34, :], 0.0), writes=[rPB[4]])
        vop(lambda e: e.memset(epsc[:, 0:1], EPS), writes=[rSmall])
        vop(lambda e: e.memset(epsc[:, 1:2], 0.5 * EPS), writes=[rSmall])
        vop(lambda e: e.memset(accW1[:, :], 0.0), writes=[rAcc1])
        vop(lambda e: e.memset(accW2[:, :], 0.0), writes=[rAcc2])
        for selt in (sel1E, sel1O, sel2E, sel2O):
            vop(lambda e, tt=selt: e.memset(tt[:, :], 0.0), writes=[rSmall])
        vop(lambda e: e.memset(sel1E[0:64, 0:1], 1.0), writes=[rSmall])
        vop(lambda e: e.memset(sel1O[64:128, 0:1], 1.0), writes=[rSmall])
        vop(lambda e: e.memset(sel2E[0:64, 1:2], 1.0), writes=[rSmall])
        vop(lambda e: e.memset(sel2O[64:128, 1:2], 1.0), writes=[rSmall])
        vop(lambda e: e.tensor_copy(relks[0:5, 64:128], relks[0:5, 0:64]),
            reads=[rSmall], writes=[rSmall])
        vop(lambda e: e.tensor_copy(relvs[0:5, 64:128], relvs[0:5, 0:64]),
            reads=[rSmall], writes=[rSmall])
        top(lambda e: e.transpose(PB[7][0:128, 0:5], relks[0:5, :],
                                  identf[0:5, 0:5]),
            reads=[rSmall], writes=[rPB[7]])
        sop(lambda e: e.copy(relkT[:, :], PB[7][0:128, 0:5]),
            reads=[rPB[7]], writes=[rSmall])
        top(lambda e: e.transpose(PB[7][0:128, 0:5], relvs[0:5, :],
                                  identf[0:5, 0:5]),
            reads=[rSmall], writes=[rPB[7]])
        sop(lambda e: e.copy(relvT[:, :], PB[7][0:128, 0:5]),
            reads=[rPB[7]], writes=[rSmall])
        for src8, dst8 in ((bq8, bqcols), (bk8, bkpcols)):
            top(lambda e, s8=src8: e.transpose(PB[7][0:128, 0:8], s8[:, :],
                                               identf[0:8, 0:8]),
                reads=[rSmall], writes=[rPB[7]])
            sop(lambda e, d8=dst8: e.copy(d8[:, :], PB[7][0:128, 0:8]),
                reads=[rPB[7]], writes=[rSmall])
        top(lambda e: e.transpose(PB[7][0:128, 0:32], b1s[:, :],
                                  identf[0:32, 0:32]),
            reads=[rSmall], writes=[rPB[7]])
        sop(lambda e: e.copy(b1cols[:, :], PB[7][0:128, 0:32]),
            reads=[rPB[7]], writes=[rSmall])
        vop(lambda e: e.tensor_scalar(bkpcols[:, :], bkpcols[:, :],
                                      relkT[:, 0:1], None, ALU.add),
            reads=[rSmall], writes=[rSmall])
        for tz, lo, hi in ((TzE, 0, 64), (TzO, 64, 128)):
            vop(lambda e, tt=tz: e.memset(tt[:, :], 0.0), writes=[rSmall])
            for col, src in ((0, 1), (1, 2)):
                vop(lambda e, tt=tz, a=lo, b=hi, c=col, s=src: e.tensor_tensor(
                    tt[a:b, c:c + 1], relkT[a:b, s:s + 1], relkT[a:b, 0:1],
                    ALU.subtract), reads=[rSmall], writes=[rSmall])
        for tv, lo, hi in ((TvBmE, 0, 64), (TvBmO, 64, 128)):
            vop(lambda e, tt=tv: e.memset(tt[:, :], 0.0), writes=[rSmall])
            for col, src in ((0, 1), (1, 2)):
                vop(lambda e, tt=tv, a=lo, b=hi, c=col, s=src: e.tensor_tensor(
                    tt[a:b, c:c + 1], relvT[a:b, s:s + 1], relvT[a:b, 0:1],
                    ALU.subtract), reads=[rSmall], writes=[rSmall])
        vop(lambda e: e.memset(Tv2zE[:, :], 0.0), writes=[rSmall])
        top(lambda e: e.transpose(PB[7][0:2, 0:128], TvBmE[:, :], identf[:, :]),
            reads=[rSmall], writes=[rPB[7]])
        sop(lambda e: e.copy(Tv2zE[0:2, :], PB[7][0:2, 0:128]),
            reads=[rPB[7]], writes=[rSmall])
        vop(lambda e: e.memset(Tv2zO[:, :], 0.0), writes=[rSmall])
        top(lambda e: e.transpose(PB[7][0:2, 0:128], TvBmO[:, :], identf[:, :]),
            reads=[rSmall], writes=[rPB[7]])
        sop(lambda e: e.copy(Tv2zO[0:2, :], PB[7][0:2, 0:128]),
            reads=[rPB[7]], writes=[rSmall])
        for dst4, eh, oh in ((sel41, sel1E, sel1O), (sel42, sel2E, sel2O),
                             (Tz4, TzE, TzO)):
            for c_, (s_, k_) in enumerate(((eh, 0), (oh, 0), (eh, 1),
                                           (oh, 1))):
                vop(lambda e, d4=dst4, s=s_, k=k_, c=c_: e.tensor_copy(
                    d4[:, c:c + 1], s[:, k:k + 1]),
                    reads=[rSmall], writes=[rSmall])
        for tv4, tvm, ca, cb in ((Tv2z4E, TvBmE, 0, 2), (Tv2z4O, TvBmO, 1, 3)):
            vop(lambda e, t4=tv4: e.memset(t4[:, :], 0.0), writes=[rSmall])
            vop(lambda e: e.memset(TvBm4[:, :], 0.0), writes=[rSmall])
            vop(lambda e, tm=tvm, c=ca: e.tensor_copy(
                TvBm4[:, c:c + 1], tm[:, 0:1]), reads=[rSmall],
                writes=[rSmall])
            vop(lambda e, tm=tvm, c=cb: e.tensor_copy(
                TvBm4[:, c:c + 1], tm[:, 1:2]), reads=[rSmall],
                writes=[rSmall])
            top(lambda e: e.transpose(PB[7][0:4, 0:128], TvBm4[:, :],
                                      identf[:, :]),
                reads=[rSmall], writes=[rPB[7]])
            sop(lambda e, t4=tv4: e.copy(t4[0:4, :], PB[7][0:4, 0:128]),
                reads=[rPB[7]], writes=[rSmall])
        for i in range(16):
            vop(lambda e, i=i: e.tensor_tensor(
                bvT0row[0:1, 64 * i:64 * i + 64],
                bvrow[0:1, 64 * i:64 * i + 64],
                relvs[0:1, 0:64], ALU.add), reads=[rSmall, rLate],
                writes=[rSmall])
        vop(lambda e: e.memset(bvVrow[0:1, :], 0.0), writes=[rSmall])
        for pp in range(8):
            vop(lambda e, pp=pp: e.tensor_copy(
                bvVrow[0:1, 192 * pp:192 * pp + 64],
                bvT0row[0:1, 128 * pp:128 * pp + 64]),
                reads=[rSmall], writes=[rSmall])
            vop(lambda e, pp=pp: e.tensor_copy(
                bvVrow[0:1, 192 * pp + 128:192 * pp + 192],
                bvT0row[0:1, 128 * pp + 64:128 * pp + 128]),
                reads=[rSmall], writes=[rSmall])
        for c in range(3):
            top(lambda e, c=c: e.matmul(PB[6][:, :], o1f[0:1, :],
                                        bvVrow[0:1, 512 * c:512 * c + 512],
                                        start=True, stop=True),
                reads=[rSmall, rLate], writes=[rPB[6]])
            sop(lambda e, c=c: e.copy(bvVb[:, 512 * c:512 * c + 512], PB[6][:, :]),
                reads=[rPB[6]], writes=[rSmall])
        for row, dstb in ((borow, bob), (b2row, b2b)):
            for c in range(2):
                top(lambda e, r=row, c=c: e.matmul(
                    PB[6][:, :], o1f[0:1, :], r[0:1, 512 * c:512 * c + 512],
                    start=True, stop=True),
                    reads=[rSmall, rLate], writes=[rPB[6]])
                sop(lambda e, d=dstb, c=c: e.copy(d[:, 512 * c:512 * c + 512],
                                                  PB[6][:, :]),
                    reads=[rPB[6]], writes=[rSmall])

        for cm in row_cm[::-1]:
            cm.__exit__(None, None, None)
        ep_cm = [nc.sbuf_tensor(f"Ep_{i}", [128, 1024], BF16) for i in range(4)]
        Ep = [cm.__enter__() for cm in ep_cm]
        stk_cms = [nc.sbuf_tensor(f"stk{i}", [128, 512], BF16)
                   for i in range(4)]
        stk1, stk2, stk3, stk4 = [cm.__enter__() for cm in stk_cms]
        rinvb_cm = [nc.sbuf_tensor(f"rinvb_{i}", [128, 512], F32) for i in range(2)]
        rinvb = [cm.__enter__() for cm in rinvb_cm]
        ab_cm = [nc.sbuf_tensor(f"ab4_{i}", [128, 4], F32) for i in range(2)]
        AB4 = [cm.__enter__() for cm in ab_cm]
        rft_cm = nc.sbuf_tensor("rft", [128, 1024], F32)
        rft = rft_cm.__enter__()
        eab_cm = nc.sbuf_tensor("eab8", [128, 8], BF16)
        eab = eab_cm.__enter__()
        cw_cm = [nc.sbuf_tensor("cwB", [128, 258], BF16)]
        cwB = cw_cm[0].__enter__()
        cw = [cwB[:, 0:129], cwB[:, 129:258]]
        rbt1_cm = nc.sbuf_tensor("rbt1", [128, 512], BF16)
        rbt1 = rbt1_cm.__enter__()

        def qk_proj(wregs, wb, wofs, biascols, dstT, rdst):
            for m in range(8):
                for n in range(4):
                    pb = (m * 4 + n) % 4
                    for k in range(ND):
                        first, last = (k == 0), (k == ND - 1)
                        sch.op(T, lambda e, pb=pb, k=k, m=m, n=n, wb=wb, wo=wofs,
                               f=first, l=last: e.matmul(
                                   PB[pb][:, :],
                                   wb[:, wo + 1024 * k + 128 * m:
                                      wo + 1024 * k + 128 * m + 128],
                                   XT[:, 2048 * k + 512 * n:
                                      2048 * k + 512 * n + 512],
                                   start=f, stop=l),
                               reads=[wregs[k], rXTk[k]],
                               writes=([rPB[pb]] if (first or last) else []))
                    sch.op(SC, lambda e, pb=pb, m=m, n=n, bc=biascols, dt=dstT:
                           e.activation(dt[:, 2048 * m + 512 * n:
                                           2048 * m + 512 * n + 512],
                                        PB[pb][:, :], AF.Identity,
                                        bias=bc[:, m:m + 1], scale=1.0),
                           reads=[rPB[pb], rSmall], writes=[rdst])

        qk_proj(rWBs, WBUF, 0, bqcols, QT, rQT)
        sch.dma(G, lambda e: e.dma_start(
            WBUF[:, :].rearrange("p (k n) -> p k n", k=8),
            d_wv.rearrange("(k p) n -> p k n", p=128)), "DWV",
            writes=list(rWBs))
        qk_proj(rWkk, VH, 8192, bkpcols, KT, rKT)
        depWk = []
        for rr_ in rWkk:
            depWk += rr_.tokens()
        rVHs = [Reg(depWk) for _ in range(NT)]
        vop(lambda e: e.memset(
            VH[:, :].rearrange("p (g c) -> p g c", c=192)[:, :, 64:65], 1.0),
            writes=list(rVHs))
        vop(lambda e: e.memset(
            VH[:, :].rearrange("p (g c) -> p g c", c=192)[:, :, 65:128], 0.0),
            writes=list(rVHs))
        for s in range(NT):
            for dd in range(2):
                pb = (s * 2 + dd) % 4
                for k in range(ND):
                    first, last = (k == 0), (k == ND - 1)
                    sch.op(T, lambda e, pb=pb, k=k, s=s, dd=dd, f=first, l=last:
                           e.matmul(
                               PB[pb][:, :],
                               XT[:, 2048 * k + 128 * s:2048 * k + 128 * s + 128],
                               WBUF[:, 1024 * k + 512 * dd:
                                    1024 * k + 512 * dd + 512],
                               start=f, stop=l),
                           reads=[rWBs[k], rXTk[k]],
                           writes=([rPB[pb]] if (first or last) else []))
                base = 1536 * s + 768 * dd
                bvof = 768 * dd
                for hh in range(2):
                    vop(lambda e, pb=pb, b=base, bo_=bvof, hh=hh:
                        e.scalar_tensor_tensor(
                            VH[:, b:b + 768].rearrange(
                                "p (g c) -> p g c", g=4)[:, :, 128 * hh:
                                                         128 * hh + 64],
                            PB[pb][:, :].rearrange(
                                "p (g c) -> p g c", g=4)[:, :, 64 * hh:
                                                         64 * hh + 64],
                            1.0,
                            bvVb[:, bo_:bo_ + 768].rearrange(
                                "p (g c) -> p g c", g=4)[:, :, 128 * hh:
                                                         128 * hh + 64],
                            ALU.mult, ALU.add),
                        reads=[rPB[pb], rSmall, rVHs[s]], writes=[rVHs[s]])
        sch.dma(G, lambda e: e.dma_start(
            WBUF[:, :].rearrange("p (k n) -> p k n", k=8),
            d_wo.rearrange("(k p) n -> p k n", p=128)), "DWO",
            writes=list(rWBs))

        depXT = []
        for rr_ in rXTk:
            depXT += rr_.tokens()
        rAOT = [[Reg(depXT) for _ in range(NT)] for _ in range(ND)]

        bg = []

        def gamma_unit(i, src, wcols, acc, racc):
            st = i % 2

            def emit():
                sch.dma("sync", lambda e, s=src, w=wcols, st=st:
                        e.dma_start(stage[st][:, 0:w], s), f"DS{st}",
                        writes=[rStage[st]])
                vop(lambda e, w=wcols, st=st, i=i: e.tensor_reduce(
                    accWall[:, i:i + 1], stage[st][:, 0:w], AX.X, ALU.add,
                    apply_absolute_value=True),
                    reads=[rStage[st]], writes=[racc])
            return emit

        gi = 0
        for k in range(ND):
            for h in range(4):
                bg.append(gamma_unit(gi, d_w1t[128 * k:128 * k + 128,
                                               1024 * h:1024 * h + 1024],
                                     1024, accW1, rAcc1))
                gi += 1
        for j in range(NF):
            bg.append(gamma_unit(gi, d_w2t[128 * j:128 * j + 128, :], 1024,
                                 accW2, rAcc2))
            gi += 1

        def gamma_finish(acc, racc, thrc, gcol):
            vop(lambda e, a=acc, g=gcol: e.tensor_reduce(
                a[:, :], accWall[:, 32 * g:32 * g + 32], AX.X, ALU.add),
                reads=[racc], writes=[racc])
            top(lambda e, a=acc: e.matmul(PB[6][0:1, 0:1], a[:, :],
                                          onescol[:, :], start=True,
                                          stop=True),
                reads=[racc, rSmall], writes=[rPB[6]])
            sop(lambda e, g=gcol: e.activation(
                gam[0:1, g:g + 1], PB[6][0:1, 0:1], AF.Identity,
                scale=1.0 / (FF * DM)), reads=[rPB[6]], writes=[rGam])
            sop(lambda e, g=gcol: e.activation(
                thrp[0:1, 2 * g:2 * g + 1], gam[0:1, g:g + 1], AF.Identity,
                scale=0.5, bias=epsc[0:1, 1:2]),
                reads=[rGam, rSmall], writes=[rThr])
            sop(lambda e, g=gcol: e.activation(
                thrp[0:1, 2 * g + 1:2 * g + 2], thrp[0:1, 2 * g:2 * g + 1],
                AF.Identity, scale=-1.0), reads=[rThr], writes=[rThr])
            top(lambda e, g=gcol: e.matmul(PB[6][0:128, 0:2], o1f[0:1, :],
                                           thrp[0:1, 2 * g:2 * g + 2],
                                           start=True, stop=True),
                reads=[rThr, rSmall, rLate], writes=[rPB[6]])
            sop(lambda e, tc=thrc: e.copy(tc[:, :], PB[6][0:128, 0:2]),
                reads=[rPB[6]], writes=[rThr])

        bg.append(lambda: gamma_finish(accW1, rAcc1, thrc1, 0))
        bg.append(lambda: gamma_finish(accW2, rAcc2, thrc2, 1))
        bgi = 0

        def drain_bg(n):
            nonlocal bgi
            while n > 0 and bgi < len(bg):
                bg[bgi]()
                bgi += 1
                n -= 1
        PBO = [0, 1]
        PSS = [2, 3, 7]

        rEabC = Reg()
        rCwB = Reg()
        rStkS = [Reg(), Reg()]
        HBv = [0, 64]
        SEL1v = [sel1E, sel1O]
        SEL2v = [sel2E, sel2O]
        TZv = [TzE, TzO]
        TV2Zv = [Tv2zE, Tv2zO]
        STKP = [(stk1, stk2), (stk3, stk4)]

        def emit_stks(pg, sl=0):
            t, pair = divmod(pg, 8)
            q0 = 512 * t
            qof = 2048 * pair
            sA, sB = STKP[sl]
            rsl = rStkS[sl]
            if t == 0:
                sch.op(G, lambda e, sA=sA: e.memset(sA[:, 0:1], 0.0),
                       writes=[rsl])
                sch.op(G, lambda e, qo=qof, sA=sA: e.tensor_tensor(
                    sA[:, 1:512], QT[:, qo + 1:qo + 512],
                    KT[:, qo + 0:qo + 511], ALU.mult),
                    reads=[rQT, rKT], writes=[rsl])
            else:
                sch.op(G, lambda e, qo=qof, q0=q0, sA=sA: e.tensor_tensor(
                    sA[:, :], QT[:, qo + q0:qo + q0 + 512],
                    KT[:, qo + q0 - 1:qo + q0 + 511], ALU.mult),
                    reads=[rQT, rKT], writes=[rsl])
            sch.op(G, lambda e, qo=qof, q0=q0, sB=sB: e.tensor_tensor(
                sB[:, :], QT[:, qo + q0:qo + q0 + 512],
                KT[:, qo + q0:qo + q0 + 512], ALU.mult),
                reads=[rQT, rKT], writes=[rsl])

        def emit_tail_pre(pg, hh):
            t, pair = divmod(pg, 8)
            pO = PBO[hh]
            rr = 64 if hh == 0 else 0
            sop(lambda e, rr=rr, pO=pO: e.activation(
                rft[rr:rr + 1, 512:1024], PB[pO][rr:rr + 1, :], AF.Ln),
                reads=[rPB[pO]], writes=[rRf[0]])
            sop(lambda e, rr=rr: e.activation(
                rbt1[rr:rr + 1, :], rft[rr:rr + 1, 512:1024], AF.Exp,
                scale=-1.0),
                reads=[rRf[0]], writes=[rRb[0]])

        def emit_tail_post(pg, hh):
            t, pair = divmod(pg, 8)
            q0 = 512 * t
            qof = 2048 * pair
            pO = PBO[hh]
            rr = 64 if hh == 0 else 0
            av0 = 0 if hh == 0 else 64
            rrvb = rRinvb[hh]
            rvb = rinvb[hh]
            top(lambda e, rr=rr: e.matmul(PB[6][:, :], oall[rr:rr + 1, 0:128],
                                          rbt1[rr:rr + 1, :],
                                          start=True, stop=True),
                reads=[rRb[0], rSmall, rLate], writes=[rPB[6]])
            vop(lambda e, rv=rvb: e.tensor_copy(rv[:, :], PB[6][:, :]),
                reads=[rPB[6]], writes=[rrvb])
            aotregs = [rAOT[pair][4 * t + i] for i in range(4)]
            vop(lambda e, av0=av0, qo=qof, q0=q0, rv=rvb, pO=pO:
                e.tensor_tensor(
                    XT[av0:av0 + 64, qo + q0:qo + q0 + 512],
                    PB[pO][av0:av0 + 64, :], rv[av0:av0 + 64, :],
                    ALU.mult),
                reads=[rPB[pO], rrvb], writes=aotregs)

        pending = []
        PGSEQ = []
        for p_ in range(8):
            PGSEQ += [24 + p_, p_]
        for p_ in range(8):
            PGSEQ += [16 + p_, 8 + p_]
        emit_stks(PGSEQ[0])
        for ui in range(4 * NPAIR):
            pg = PGSEQ[ui]
            t, pair = divmod(pg, 8)
            q0 = 512 * t
            nkt = 4 * t + 4
            qof = 2048 * pair
            sl = ui % 2
            sA, sB = STKP[sl]
            rsl = rStkS[sl]
            drain_bg(3)
            PO = [PBO[hh] for hh in range(2)]
            if ui + 1 < 4 * NPAIR:
                emit_stks(PGSEQ[ui + 1], (ui + 1) % 2)
            for f in pending:
                f[0]()

            top(lambda e, sA=sA: e.matmul(
                PB[4][0:4, :], sel41[:, :], sA[:, :],
                start=True, stop=False),
                reads=[rsl], writes=[rPB[4]])
            top(lambda e, sB=sB: e.matmul(
                PB[4][0:4, :], sel42[:, :], sB[:, :],
                start=False, stop=False),
                reads=[rsl], writes=[])
            top(lambda e, qo=qof, q0=q0: e.matmul(
                PB[4][0:4, :], Tz4[:, :], QT[:, qo + q0:qo + q0 + 512],
                start=False, stop=True),
                reads=[rQT, rSmall], writes=[rPB[4]])
            sop(lambda e: e.activation(
                E12[0][0:4, :], PB[4][0:4, :], AF.Exp, scale=SCALE),
                reads=[rPB[4]], writes=[rE12[0]])
            if t == 0:
                vop(lambda e: e.memset(E12[0][0:2, 0:1], 0.0),
                    writes=[rE12[0]])

            def emit_av(hh, kt):
                j = kt - 4 * t
                c0 = 128 * j if j > 0 else 0
                vofs = 1536 * kt + 192 * pair + 64 * hh
                first = (kt == 0)
                pO = PO[hh]
                ei = kt % 4
                sch.op(T, lambda e, c0=c0, vo=vofs, f=first, pO=pO, ei=ei,
                       hh=hh: e.matmul(
                           PB[pO][:, c0:512],
                           VH[:, vo:vo + 128],
                           Ep[ei][:, 512 * hh + c0:512 * hh + 512],
                           start=f, stop=False),
                       reads=[rVHs[kt], rE[ei]],
                       writes=([rPB[pO]] if first else []))

            for kt in range(nkt):
                j = kt - 4 * t
                c0 = 128 * j if j > 0 else 0
                kof = qof + 128 * kt
                spill = (t > 0 and kt == 4 * t - 1)
                ei = kt % 4
                for hh in range(2):
                    hb = HBv[hh]
                    pS = PSS[(2 * kt + hh) % 3]
                    sch.op(T, lambda e, kof=kof, c0=c0, hb=hb, hh=hh, pS=pS,
                           qo=qof, q0=q0: e.matmul(
                               PB[pS][:, c0:512],
                               KT[hb:hb + 64, kof:kof + 128],
                               QT[hb:hb + 64, qo + q0 + c0:qo + q0 + 512],
                               start=True, stop=True),
                           reads=[rKT, rQT], writes=[rPB[pS]])
                if spill or j >= 0:
                    w0 = qof + q0 + c0 if j >= 0 else None
                    for hh in range(2):
                        hb = HBv[hh]
                        if spill:
                            top(lambda e, hb=hb, qo=qof, q0=q0, tz=TZv[hh],
                                hh=hh: e.matmul(
                                    PB[5][:, 4 * hh:4 * hh + 2],
                                    QT[hb:hb + 64,
                                       qo + q0 - 127:qo + q0 + 1],
                                    tz[hb:hb + 64, :],
                                    start=True, stop=True),
                                reads=[rQT, rSmall], writes=[rPB[5]])
                        else:
                            top(lambda e, hb=hb, w0=w0, tz=TZv[hh], hh=hh:
                                e.matmul(
                                    PB[5][:, 4 * hh:4 * hh + 2],
                                    QT[hb:hb + 64, w0:w0 + 128],
                                    tz[hb:hb + 64, :],
                                    start=True, stop=True),
                                reads=[rQT, rSmall], writes=[rPB[5]])
                            wid = 127 if j == 3 else 128
                            top(lambda e, hb=hb, w0=w0, tz=TZv[hh], wd=wid,
                                hh=hh: e.matmul(
                                    PB[5][0:wd, 4 * hh + 2:4 * hh + 4],
                                    QT[hb:hb + 64, w0 + 1:w0 + 1 + wd],
                                    tz[hb:hb + 64, :],
                                    start=True, stop=True),
                                reads=[rQT, rSmall], writes=[rPB[5]])
                    sop(lambda e: e.activation(eab[:, :], PB[5][:, 0:8],
                                               AF.Exp, scale=SCALE),
                        reads=[rPB[5]], writes=[rEabC])
                for hh in range(2):
                    pS = PSS[(2 * kt + hh) % 3]
                    sop(lambda e, ei=ei, hh=hh, c0=c0, pS=pS: e.activation(
                        Ep[ei][:, 512 * hh + c0:512 * hh + 512],
                        PB[pS][:, c0:512], AF.Exp, scale=SCALE),
                        reads=[rPB[pS]], writes=[rE[ei]])
                if j >= 0:
                    wde = 128 if j == 3 else 129
                    for hh in range(2):
                        vop(lambda e, hh=hh, wde=wde: e.scalar_tensor_tensor(
                            cw[hh][:, 0:wde], idd129[:, 0:wde],
                            eab[:, 4 * hh + 1:4 * hh + 2], m0c[:, 0:wde],
                            ALU.mult, ALU.add),
                            reads=[rEabC, rSmall, rLate], writes=[rCwB])
                        vop(lambda e, hh=hh, wde=wde: e.scalar_tensor_tensor(
                            cw[hh][:, 0:wde], identu[:, 0:wde],
                            eab[:, 4 * hh + 2:4 * hh + 3], cw[hh][:, 0:wde],
                            ALU.mult, ALU.add),
                            reads=[rEabC, rSmall, rLate], writes=[rCwB])
                    vop(lambda e, ei=ei, c0=c0, wde=wde: e.tensor_tensor(
                        Ep[ei][:, :].rearrange(
                            "p (g c) -> p g c", g=2)[:, :, c0:c0 + wde],
                        Ep[ei][:, :].rearrange(
                            "p (g c) -> p g c", g=2)[:, :, c0:c0 + wde],
                        cwB[:, :].rearrange(
                            "p (g c) -> p g c", g=2)[:, :, 0:wde],
                        ALU.mult),
                        reads=[rCwB], writes=[rE[ei]])
                if spill:
                    for hh in range(2):
                        vop(lambda e, hh=hh: e.scalar_tensor_tensor(
                            cw[hh][:, 0:1], elast[:, :],
                            eab[:, 4 * hh:4 * hh + 1], onesxl[:, :],
                            ALU.mult, ALU.add),
                            reads=[rEabC, rSmall, rLate], writes=[rCwB])
                    vop(lambda e, ei=ei: e.tensor_tensor(
                        Ep[ei][:, :].rearrange(
                            "p (g c) -> p g c", g=2)[:, :, 0:1],
                        Ep[ei][:, :].rearrange(
                            "p (g c) -> p g c", g=2)[:, :, 0:1],
                        cwB[:, :].rearrange(
                            "p (g c) -> p g c", g=2)[:, :, 0:1],
                        ALU.mult),
                        reads=[rCwB], writes=[rE[ei]])
                if kt == 1:
                    while pending:
                        pending.pop(0)[1]()
                if kt > 2:
                    emit_av(0, kt - 3)
                    emit_av(1, kt - 3)
            for kk in (nkt - 3, nkt - 2, nkt - 1):
                emit_av(0, kk)
                emit_av(1, kk)
            TV4 = [Tv2z4E, Tv2z4O]
            for hh in range(2):
                top(lambda e, tv=TV4[hh], pO=PO[hh]: e.matmul(
                    PB[pO][:, :], tv[:, :], E12[0][:, :], start=False,
                    stop=True),
                    reads=[rSmall, rE12[0]], writes=[rPB[PO[hh]]])
                pending.append((lambda pg=pg, hh=hh: emit_tail_pre(pg, hh),
                                lambda pg=pg, hh=hh: emit_tail_post(pg, hh)))
        for f in pending:
            f[0]()
        while pending:
            pending.pop(0)[1]()

        depQT = rQT.tokens()
        depVH = list(depWk)
        for rr_ in rVHs:
            depVH += rr_.tokens()
        depKT = rKT.tokens()
        depB = list(depQT) + list(depVH)
        for rr_ in (rE + [rStk] + rRinvb + rAB4 + rRf + rRb + [rAbsp]):
            depB += rr_.tokens()
        rbt1_cm.__exit__(None, None, None)
        for c in cw_cm[::-1]:
            c.__exit__(None, None, None)
        eab_cm.__exit__(None, None, None)
        rft_cm.__exit__(None, None, None)
        for cm in ab_cm[::-1]:
            cm.__exit__(None, None, None)
        for cm in rinvb_cm[::-1]:
            cm.__exit__(None, None, None)
        for cm in stk_cms[::-1]:
            cm.__exit__(None, None, None)
        for cm in ep_cm[::-1]:
            cm.__exit__(None, None, None)
        vh_cm.__exit__(None, None, None)
        qt_cm.__exit__(None, None, None)

        W1F8 = E(nc.sbuf_tensor("W1F8", [128, 32768], FP8))
        W2F8 = E(nc.sbuf_tensor("W2F8", [128, 32768], FP8))
        zbuf = [E(nc.sbuf_tensor(f"zbuf_{i}", [128, DM], F32)) for i in range(2)]
        xres = [E(nc.sbuf_tensor(f"xres_{i}", [128, DM], F32)) for i in range(2)]
        pscr = [E(nc.sbuf_tensor(f"pscr_{i}", [128, 1024], BF16))
                for i in range(2)]
        rW1 = [[Reg(depB) for _ in range(4)] for _ in range(ND)]
        rW2 = [Reg(depB) for _ in range(NF)]
        rX1N = [Reg(depKT) for _ in range(NT)]
        rZ = [Reg(depB), Reg(depB)]
        rXres = [Reg(depB), Reg(depB)]
        rPscr = Reg(depB)

        w1units = []
        for hf in range(4):
            for k in range(ND):
                w1units.append(("sign",
                                d_w1t[128 * k:128 * k + 128,
                                      1024 * hf:1024 * hf + 1024],
                                1024, W1F8, 4096 * k + 1024 * hf, thrc1,
                                rW1[k][hf]))
        w2units = [("dve", d_w2t[128 * j:128 * j + 128, :], 1024,
                    W2F8, 1024 * j, thrc2, rW2[j]) for j in range(NF)]
        qorder = w1units + w2units
        qunits = [(qi % 2,) + u for qi, u in enumerate(qorder)]
        qidx = 0

        def quant_unit(st, kind, src, wcols, dst, dcol, thrc, rdst):
            sch.dma("sync", lambda e, s=src, w=wcols, st=st:
                    e.dma_start(stage[st][:, 0:w], s), f"DS{st}",
                    writes=[rStage[st]])
            if kind == "sign":
                sop(lambda e, w=wcols, st=st, tc=thrc: e.activation(
                    pscr[0][:, 0:w], stage[st][:, 0:w], AF.Sign,
                    bias=tc[:, 1:2], scale=1.0),
                    reads=[rStage[st], rThr], writes=[rPscr])
                sop(lambda e, w=wcols, st=st, tc=thrc: e.activation(
                    pscr[1][:, 0:w], stage[st][:, 0:w], AF.Sign,
                    bias=tc[:, 0:1], scale=1.0),
                    reads=[rStage[st], rThr], writes=[rPscr])
                vop(lambda e, w=wcols, d=dst, c=dcol: e.tensor_tensor(
                    d[:, c:c + w], pscr[0][:, 0:w], pscr[1][:, 0:w],
                    ALU.add),
                    reads=[rPscr], writes=[rdst])
            else:
                vop(lambda e, w=wcols, st=st, tc=thrc: e.tensor_scalar(
                    pscr[0][:, 0:w], stage[st][:, 0:w], tc[:, 0:1], None,
                    ALU.is_ge),
                    reads=[rStage[st], rThr], writes=[rPscr])
                vop(lambda e, w=wcols, st=st, tc=thrc: e.tensor_scalar(
                    pscr[1][:, 0:w], stage[st][:, 0:w], tc[:, 1:2], None,
                    ALU.is_le),
                    reads=[rStage[st], rThr], writes=[rPscr])
                vop(lambda e, w=wcols, d=dst, c=dcol: e.tensor_tensor(
                    d[:, c:c + w], pscr[0][:, 0:w], pscr[1][:, 0:w],
                    ALU.subtract),
                    reads=[rPscr], writes=[rdst])

        def drain_quant(n):
            nonlocal qidx
            while n > 0 and qidx < len(qunits):
                quant_unit(*qunits[qidx])
                qidx += 1
                n -= 1

        sch.dma(G, lambda e: e.dma_start(xres[0][:, :], d_x[0:128, :]),
                "DX0", writes=[rXres[0]])
        opend = []
        for s in range(NT):
            drain_bg(99)
            drain_quant(2)
            if s + 1 < NT:
                sch.dma(G, lambda e, s=s: e.dma_start(
                    xres[(s + 1) % 2][:, :],
                    d_x[128 * (s + 1):128 * (s + 2), :]),
                    f"DX{(s + 1) % 2}", writes=[rXres[(s + 1) % 2]])
            zb = zbuf[s % 2]
            rz = rZ[s % 2]
            rx = rXres[s % 2]
            for dd in range(2):
                pb = (2 * s + dd) % 4
                for k in range(ND):
                    first = (k == 0)
                    sch.op(T, lambda e, pb=pb, k=k, s=s, dd=dd, f=first:
                           e.matmul(
                               PB[pb][:, :],
                               XT[:, 2048 * k + 128 * s:2048 * k + 128 * s + 128],
                               WBUF[:, 1024 * k + 512 * dd:
                                    1024 * k + 512 * dd + 512],
                               start=f, stop=False),
                           reads=[rWBs[k], rAOT[k][s]],
                           writes=([rPB[pb]] if first else []))
                sch.op(T, lambda e, pb=pb, dd=dd: e.matmul(
                    PB[pb][:, :], inv128[:, :],
                    bob[:, 512 * dd:512 * dd + 512], start=False, stop=True),
                    reads=[rSmall, rLate], writes=[rPB[pb]])
                vop(lambda e, pb=pb, dd=dd, zb=zb, xr=xres[s % 2]:
                    e.scalar_tensor_tensor(
                        zb[:, 512 * dd:512 * dd + 512],
                        xr[:, 512 * dd:512 * dd + 512], 1.0, PB[pb][:, :],
                        ALU.mult, ALU.add),
                    reads=[rx, rPB[pb]], writes=[rz])
            if opend:
                opend.pop(0)()
            par = s % 2
            rB = rBnP[par]
            vop(lambda e, zb=zb, p=par: e.bn_stats(
                bnst[:, 12 * p:12 * p + 6], zb[:, 0:512]),
                reads=[rz], writes=[rB])
            vop(lambda e, zb=zb, p=par: e.bn_stats(
                bnst[:, 12 * p + 6:12 * p + 12], zb[:, 512:1024]),
                reads=[rz], writes=[rB])
            vop(lambda e, p=par: e.bn_aggr(bnagg[:, 2 * p:2 * p + 2],
                                           bnst[:, 12 * p:12 * p + 12]),
                reads=[], writes=[rB])
            sop(lambda e, p=par: e.activation(
                sdcol[:, p:p + 1], bnagg[:, 2 * p + 1:2 * p + 2], AF.Sqrt,
                bias=epsc[:, 0:1]),
                reads=[rSmall], writes=[rB])
            vop(lambda e, p=par: e.reciprocal(rlncol[:, p:p + 1],
                                              sdcol[:, p:p + 1]),
                reads=[], writes=[rB])
            vop(lambda e, zb=zb, p=par: e.tensor_scalar(
                zb[:, :], zb[:, :], bnagg[:, 2 * p:2 * p + 1],
                rlncol[:, p:p + 1],
                ALU.subtract, ALU.mult), reads=[rB], writes=[rz])
            vop(lambda e, zb=zb, s=s: e.tensor_copy(
                KT[:, 1024 * s:1024 * s + 1024], zb[:, :]),
                reads=[rz], writes=[rX1N[s]])

            def emit_x1t(s=s, zb=zb, rz=rz):
                for g in range(2):
                    for q in range(4):
                        k = 4 * g + q
                        sch.op(T, lambda e, q=q, k=k, zb=zb: e.transpose(
                            PB[7][:, 128 * q:128 * q + 128],
                            zb[:, 128 * k:128 * k + 128], identf[:, :]),
                            reads=[rz], writes=[rPB[7]])
                    for q in range(4):
                        k = 4 * g + q
                        sop(lambda e, q=q, k=k, s=s: e.activation(
                            XT[:, 2048 * k + 128 * s:2048 * k + 128 * s + 128],
                            PB[7][:, 128 * q:128 * q + 128], AF.Identity,
                            scale=0.5),
                            reads=[rPB[7]], writes=[rAOT[k][s]])
            opend.append(emit_x1t)

        while opend:
            opend.pop(0)()
        wbuf_cm.__exit__(None, None, None)
        depWB = []
        for rr_ in rWBs:
            depWB += rr_.tokens()
        outst = [CM.enter_context(
            nc.sbuf_tensor(f"outst_{i}", [128, DM], F32, side="right"))
            for i in range(2)]
        hbuf = [CM.enter_context(
            nc.sbuf_tensor(f"hbuf_{i}", [128, 256], BF16, side="right"))
            for i in range(8)]
        rOut = [Reg(depWB), Reg(depWB)]
        rHb = [Reg(depWB) for _ in range(8)]

        OUT2 = [(PB[0][:, :], rPB[0]), (PB[1][:, :], rPB[1]),
                (PB[2][:, :], rPB[2]), (PB[3][:, :], rPB[3])]

        def emit_ffn2(blk, j):
            hb = hbuf[j % 8]
            rhb = rHb[j % 8]
            for sub in range(2):
                for dd in range(2):
                    po = 2 * sub + dd
                    oap, oreg = OUT2[po]
                    first = (j == 0)
                    sch.op(T, lambda e, oap=oap, sub=sub, dd=dd, j=j, hb=hb,
                           f=first: e.matmul(
                               oap,
                               hb[:, 128 * sub:128 * sub + 128],
                               W2F8[:, 1024 * j + 512 * dd:
                                    1024 * j + 512 * dd + 512],
                               start=f, stop=False),
                           reads=[rhb, rW2[j]],
                           writes=([oreg] if first else []))
                    if j == NF - 1:
                        sch.op(T, lambda e, oap=oap, dd=dd: e.matmul(
                            oap, inv128[:, :],
                            b2b[:, 512 * dd:512 * dd + 512],
                            start=False, stop=True),
                            reads=[rSmall, rLate], writes=[oreg])

        for blk in range(8):
            for j in range(NF):
                if blk == 0:
                    drain_quant(1)
                ph = 4 + (j % 2)
                rhb = rHb[j % 8]
                hb = hbuf[j % 8]
                for k in range(ND):
                    first, last = (k == 0), (k == ND - 1)
                    sch.op(T, lambda e, ph=ph, k=k, j=j, blk=blk, f=first,
                           l=last: e.matmul(
                               PB[ph][:, 0:256],
                               W1F8[:, 4096 * k + 128 * j:
                                    4096 * k + 128 * j + 128],
                               XT[:, 2048 * k + 256 * blk:
                                  2048 * k + 256 * blk + 256],
                               start=f, stop=l),
                           reads=[rW1[k][j // 8], rAOT[k][2 * blk],
                                  rAOT[k][2 * blk + 1]],
                           writes=([rPB[ph]] if (first or last) else []))
                sop(lambda e, ph=ph, j=j, hb=hb: e.activation(
                    hb[:, :], PB[ph][:, 0:256], AF.Relu,
                    bias=b1cols[:, j:j + 1], scale=1.0),
                    reads=[rPB[ph], rSmall], writes=[rhb])
                if j >= 5:
                    emit_ffn2(blk, j - 5)
            for jj in range(NF - 5, NF):
                emit_ffn2(blk, jj)
            for sub in range(2):
                s = 2 * blk + sub
                zb = zbuf[sub]
                rz = rZ[sub]
                for dd in range(2):
                    po = 2 * sub + dd
                    oap, oreg = OUT2[po]
                    vop(lambda e, oap=oap, dd=dd, zb=zb, s=s:
                        e.scalar_tensor_tensor(
                            zb[:, 512 * dd:512 * dd + 512],
                            KT[:, 1024 * s + 512 * dd:
                               1024 * s + 512 * dd + 512],
                            1.0, oap, ALU.mult, ALU.add),
                        reads=[rX1N[s], oreg], writes=[rz])
                par = sub
                rB = rBnP[par]
                vop(lambda e, zb=zb, p=par: e.bn_stats(
                    bnst[:, 12 * p:12 * p + 6], zb[:, 0:512]),
                    reads=[rz], writes=[rB])
                vop(lambda e, zb=zb, p=par: e.bn_stats(
                    bnst[:, 12 * p + 6:12 * p + 12], zb[:, 512:1024]),
                    reads=[rz], writes=[rB])
                vop(lambda e, p=par: e.bn_aggr(bnagg[:, 2 * p:2 * p + 2],
                                               bnst[:, 12 * p:12 * p + 12]),
                    reads=[], writes=[rB])
                sop(lambda e, p=par: e.activation(
                    sdcol[:, p:p + 1], bnagg[:, 2 * p + 1:2 * p + 2], AF.Sqrt,
                    bias=epsc[:, 0:1]),
                    reads=[rSmall], writes=[rB])
                vop(lambda e, p=par: e.reciprocal(rlncol[:, p:p + 1],
                                                  sdcol[:, p:p + 1]),
                    reads=[], writes=[rB])
                vop(lambda e, zb=zb, o=outst[sub], p=par: e.tensor_scalar(
                    o[:, :], zb[:, :], bnagg[:, 2 * p:2 * p + 1],
                    rlncol[:, p:p + 1],
                    ALU.subtract, ALU.mult),
                    reads=[rBnP[par], rz], writes=[rOut[sub]])
                sch.dma("sync", lambda e, s=s, o=outst[sub]: e.dma_start(
                    d_out[128 * s:128 * s + 128, :], o[:, :]),
                    f"DO{sub}", reads=[rOut[sub]])

        waited = {s_: set() for s_ in sch.n}
        for eng_name in ENGS:
            for waits, fn, sem, amt, val in sch.ops[eng_name]:
                for s_, v in waits:
                    waited[s_].add(v)
        for s_ in sch.n:
            if sch.n[s_] > 0:
                waited[s_].add(sch.n[s_])
                if sch.is_dma_sem(s_):
                    for v_ in range(16, sch.n[s_] + 1, 16):
                        waited[s_].add(v_)
        remap = {}
        newtot = {}
        for s_, vals in waited.items():
            amt_ = 16 if sch.is_dma_sem(s_) else 1
            for rank, v in enumerate(sorted(vals)):
                remap[(s_, v)] = (rank + 1) * amt_
            newtot[s_] = len(vals) * amt_

        with nc.Block() as block:
            def mk(eng_name):
                def body(e):
                    for waits, fn, sem, amt, val in sch.ops[eng_name]:
                        for s_, v in waits:
                            e.wait_ge(sems[s_], remap[(s_, v)])
                        inst = fn(e)
                        if inst is not None and val in waited[sem]:
                            inst.then_inc(sems[sem], amt)
                    if eng_name == "sync":
                        for s_ in sorted(sch.n):
                            if sch.is_dma_sem(s_) and sch.n[s_] > 0:
                                e.wait_ge(sems[s_], newtot[s_])
                return body

            block.tensor(mk("tensor"))
            block.scalar(mk("scalar"))
            block.vector(mk("vector"))
            block.gpsimd(mk("gpsimd"))
            block.sync(mk("sync"))

    return nc


_CACHE = {}


def _consts():
    ident = np.eye(128, dtype=np.float32)
    identu = np.zeros((128, 129), dtype=ml_dtypes.bfloat16)
    for p in range(128):
        identu[p, p + 1] = 1.0
    elast = np.zeros((128, 1), dtype=np.float32)
    elast[127, 0] = 1.0
    idd = np.zeros((128, 129), dtype=ml_dtypes.bfloat16)
    m0 = np.zeros((128, 129), dtype=ml_dtypes.bfloat16)
    for p in range(128):
        idd[p, p] = 1.0
        m0[p, p + 2:] = 1.0
    oxl = np.ones((128, 1), dtype=ml_dtypes.bfloat16)
    oxl[127, 0] = 0.0
    return dict(
        identf=ident, idd129=idd, identu129=identu, m0c129=m0, onesxl=oxl,
        elast=elast,
        onesrow_f=np.ones((1, 128), dtype=np.float32),
        onesrow_b=np.ones((1, 128), dtype=ml_dtypes.bfloat16),
        onesall=np.ones((128, 128), dtype=ml_dtypes.bfloat16),
        inv128=np.full((128, 128), 1.0 / 128, dtype=ml_dtypes.bfloat16),
    )


def kernel(**inputs):
    x = np.ascontiguousarray(np.asarray(inputs["L0_attn_x"], dtype=np.float32))
    wq = np.ascontiguousarray(np.asarray(inputs["L0_attn_W_q"], dtype=np.float32))
    wk = np.ascontiguousarray(np.asarray(inputs["L0_attn_W_k"], dtype=np.float32))
    wv = np.ascontiguousarray(np.asarray(inputs["L0_attn_W_v"], dtype=np.float32))
    wo = np.ascontiguousarray(np.asarray(inputs["L0_attn_W_o"], dtype=np.float32))
    bq = np.ascontiguousarray(np.asarray(inputs["L0_attn_b_q"],
                                         dtype=np.float32).reshape(8, 128))
    bk = np.ascontiguousarray(np.asarray(inputs["L0_attn_b_k"],
                                         dtype=np.float32).reshape(8, 128))
    bv = np.ascontiguousarray(np.asarray(inputs["L0_attn_b_v"],
                                         dtype=np.float32).reshape(1, 1024))
    bo = np.ascontiguousarray(np.asarray(inputs["L0_attn_b_o"],
                                         dtype=np.float32).reshape(1, 1024))
    rk = np.ascontiguousarray(np.asarray(inputs["L0_attn_rel_k_table"],
                                         dtype=np.float32))
    rv = np.ascontiguousarray(np.asarray(inputs["L0_attn_rel_v_table"],
                                         dtype=np.float32))
    w1t = np.ascontiguousarray(np.asarray(inputs["L0_ffn_W1"],
                                          dtype=np.float32).T)
    b1 = np.ascontiguousarray(np.asarray(inputs["L0_ffn_b1"],
                                         dtype=np.float32).reshape(32, 128))
    w2t = np.ascontiguousarray(np.asarray(inputs["L0_ffn_W2"],
                                          dtype=np.float32).T)
    b2 = np.ascontiguousarray(np.asarray(inputs["L0_ffn_b2"],
                                         dtype=np.float32).reshape(1, 1024))

    if "nc" not in _CACHE:
        _CACHE["nc"] = build_program()
    nc = _CACHE["nc"]
    consts = _consts()
    in_maps = []
    for b in range(8):
        m = dict(
            x_nat=x[b], xT=np.ascontiguousarray(x[b].T),
            Wq=wq, Wk=wk, Wv=wv, Wo=wo, W1T=w1t, W2T=w2t,
            bq8=bq, bk8=bk, b1_32=b1, bv_row=bv, bo_row=bo, b2_row=b2,
            relk=rk, relv=rv,
        )
        m.update(consts)
        in_maps.append(m)
    res = run_bass_kernel_spmd(nc, in_maps, core_ids=list(range(8)))
    return np.stack([np.asarray(res.results[b]["out"]) for b in range(8)], axis=0)


# version 73
# speedup vs baseline: 1.1742x; 1.1742x over previous
import numpy as np
import ml_dtypes
import concourse.bass as bass
import concourse.mybir as mybir
from concourse.bass_utils import run_bass_kernel_spmd
from concourse import dve_ops

F32 = mybir.dt.float32
BF16 = mybir.dt.bfloat16
FP8 = mybir.dt.float8e4
AF = mybir.ActivationFunctionType
ALU = mybir.AluOpType
AX = mybir.AxisListType

S = 2048
DM = 1024
NPAIR = 8
FF = 4096
NT = 16
ND = 8
NQ = 4
NF = 32
SCALE = 0.125
EPS = 1e-5

ENGS = ("tensor", "scalar", "vector", "gpsimd", "sync")


class Reg:
    __slots__ = ("w", "rs")

    def __init__(self, deps=()):
        self.w = None
        self.rs = list(deps)

    def tokens(self):
        t = list(self.rs)
        if self.w:
            t.append(self.w)
        return t


class Sch:
    SEM_OF = {"tensor": "T", "scalar": "S", "vector": "V", "gpsimd": "G"}

    def __init__(self):
        self.ops = {e: [] for e in ENGS}
        self.n = {s: 0 for s in ("T", "S", "V", "G")}
        self.cw = {}

    def is_dma_sem(self, s):
        return s.startswith("D")

    def _emit(self, eng, fn, reads, writes, sem, amt):
        need = {}
        for r in reads:
            if r.w is not None:
                s, v = r.w
                if need.get(s, 0) < v:
                    need[s] = v
        for wr in writes:
            for s, v in wr.tokens():
                if need.get(s, 0) < v:
                    need[s] = v
        waits = []
        for s, v in need.items():
            if eng == "tensor" and s == "T":
                continue
            if self.cw.get((eng, s), 0) >= v:
                continue
            waits.append((s, v))
            self.cw[(eng, s)] = v
        self.n[sem] += amt
        tok = (sem, self.n[sem])
        self.ops[eng].append((waits, fn, sem, amt, self.n[sem]))
        for r in reads:
            r.rs.append(tok)
        for wr in writes:
            wr.w = tok
            wr.rs = []
        return tok

    def op(self, eng, fn, reads=(), writes=()):
        return self._emit(eng, fn, reads, writes, self.SEM_OF[eng], 1)

    def dma(self, eng, fn, dsem, reads=(), writes=()):
        self.n.setdefault(dsem, 0)
        return self._emit(eng, fn, reads, writes, dsem, 16)


def build_program():
    nc = bass.Bass("TRN2", target_bir_lowering=False)
    sch = Sch()
    T, SC, V, G = "tensor", "scalar", "vector", "gpsimd"

    d_x = nc.dram_tensor("x_nat", [S, DM], F32, kind="ExternalInput")
    d_xT = nc.dram_tensor("xT", [DM, S], F32, kind="ExternalInput")
    d_wq = nc.dram_tensor("Wq", [DM, DM], F32, kind="ExternalInput")
    d_wk = nc.dram_tensor("Wk", [DM, DM], F32, kind="ExternalInput")
    d_wv = nc.dram_tensor("Wv", [DM, DM], F32, kind="ExternalInput")
    d_wo = nc.dram_tensor("Wo", [DM, DM], F32, kind="ExternalInput")
    d_w1t = nc.dram_tensor("W1T", [DM, FF], F32, kind="ExternalInput")
    d_w2t = nc.dram_tensor("W2T", [FF, DM], F32, kind="ExternalInput")
    d_bq = nc.dram_tensor("bq8", [8, 128], F32, kind="ExternalInput")
    d_bk = nc.dram_tensor("bk8", [8, 128], F32, kind="ExternalInput")
    d_b1 = nc.dram_tensor("b1_32", [32, 128], F32, kind="ExternalInput")
    d_bv = nc.dram_tensor("bv_row", [1, DM], F32, kind="ExternalInput")
    d_bo = nc.dram_tensor("bo_row", [1, DM], F32, kind="ExternalInput")
    d_b2 = nc.dram_tensor("b2_row", [1, DM], F32, kind="ExternalInput")
    d_rk = nc.dram_tensor("relk", [5, 64], F32, kind="ExternalInput")
    d_rv = nc.dram_tensor("relv", [5, 64], F32, kind="ExternalInput")
    d_idf = nc.dram_tensor("identf", [128, 128], F32, kind="ExternalInput")
    d_idb = nc.dram_tensor("idd129", [128, 129], BF16, kind="ExternalInput")
    d_idu = nc.dram_tensor("identu129", [128, 129], BF16, kind="ExternalInput")
    d_tng = nc.dram_tensor("m0c129", [128, 129], BF16, kind="ExternalInput")
    d_oxl = nc.dram_tensor("onesxl", [128, 1], BF16, kind="ExternalInput")
    d_els = nc.dram_tensor("elast", [128, 1], F32, kind="ExternalInput")
    d_o1f = nc.dram_tensor("onesrow_f", [1, 128], F32, kind="ExternalInput")
    d_o1b = nc.dram_tensor("onesrow_b", [1, 128], BF16, kind="ExternalInput")
    d_oall = nc.dram_tensor("onesall", [128, 128], BF16, kind="ExternalInput")
    d_i128 = nc.dram_tensor("inv128", [128, 128], BF16, kind="ExternalInput")
    d_out = nc.dram_tensor("out", [S, DM], F32, kind="ExternalOutput")

    from contextlib import ExitStack
    with ExitStack() as CM:
        E = CM.enter_context
        identf = E(nc.sbuf_tensor("identf_sb", [128, 128], F32))
        idd129 = E(nc.sbuf_tensor("idd129_sb", [128, 129], BF16))
        identu = E(nc.sbuf_tensor("identu_sb", [128, 129], BF16))
        m0c = E(nc.sbuf_tensor("m0c_sb", [128, 129], BF16))
        onesxl = E(nc.sbuf_tensor("onesxl_sb", [128, 1], BF16))
        elast = E(nc.sbuf_tensor("elast_sb", [128, 1], F32))
        o1f = E(nc.sbuf_tensor("o1f_sb", [1, 128], F32))
        o1b = E(nc.sbuf_tensor("o1b_sb", [1, 128], BF16))
        oall = E(nc.sbuf_tensor("oall_sb", [128, 128], BF16))
        inv128 = E(nc.sbuf_tensor("inv128_sb", [128, 128], BF16))
        bq8 = E(nc.sbuf_tensor("bq8_sb", [8, 128], F32))
        bk8 = E(nc.sbuf_tensor("bk8_sb", [8, 128], F32))
        b1s = E(nc.sbuf_tensor("b1s_sb", [32, 128], F32))
        relks = E(nc.sbuf_tensor("relks_sb", [5, 128], F32))
        relvs = E(nc.sbuf_tensor("relvs_sb", [5, 128], F32))
        relkT = E(nc.sbuf_tensor("relkT_sb", [128, 5], F32))
        relvT = E(nc.sbuf_tensor("relvT_sb", [128, 5], F32))
        bqcols = E(nc.sbuf_tensor("bqcols", [128, 8], F32))
        bkpcols = E(nc.sbuf_tensor("bkpcols", [128, 8], F32))
        b1cols = E(nc.sbuf_tensor("b1cols", [128, 32], F32))
        TzE = E(nc.sbuf_tensor("TzE", [128, 2], BF16))
        TzO = E(nc.sbuf_tensor("TzO", [128, 2], BF16))
        TvBmE = E(nc.sbuf_tensor("TvBmE", [128, 2], F32))
        TvBmO = E(nc.sbuf_tensor("TvBmO", [128, 2], F32))
        TvBm34 = E(nc.sbuf_tensor("TvBm34", [128, 34], F32))
        Tv2zE = E(nc.sbuf_tensor("Tv2zE", [128, 128], BF16))
        Tv2zO = E(nc.sbuf_tensor("Tv2zO", [128, 128], BF16))
        sel1E = E(nc.sbuf_tensor("sel1E", [128, 2], BF16))
        sel1O = E(nc.sbuf_tensor("sel1O", [128, 2], BF16))
        sel2E = E(nc.sbuf_tensor("sel2E", [128, 2], BF16))
        sel2O = E(nc.sbuf_tensor("sel2O", [128, 2], BF16))
        sel41 = E(nc.sbuf_tensor("sel41", [128, 4], BF16))
        sel42 = E(nc.sbuf_tensor("sel42", [128, 4], BF16))
        Tz4 = E(nc.sbuf_tensor("Tz4", [128, 4], BF16))
        TvBm4 = E(nc.sbuf_tensor("TvBm4", [128, 4], F32))
        Tv2z4E = E(nc.sbuf_tensor("Tv2z4E", [128, 128], BF16))
        Tv2z4O = E(nc.sbuf_tensor("Tv2z4O", [128, 128], BF16))
        bvVb = E(nc.sbuf_tensor("bvVb", [128, 1536], BF16))
        bob = E(nc.sbuf_tensor("bo_bcast", [128, DM], BF16))
        b2b = E(nc.sbuf_tensor("b2_bcast", [128, DM], BF16))
        onescol = E(nc.sbuf_tensor("onescol", [128, 1], F32))
        accW1 = E(nc.sbuf_tensor("accW1", [128, 1], F32))
        accW2 = E(nc.sbuf_tensor("accW2", [128, 1], F32))
        gtmp = E(nc.sbuf_tensor("gtmp", [128, 1], F32))
        accWall = E(nc.sbuf_tensor("accWall", [128, 64], F32))
        gam = E(nc.sbuf_tensor("gam", [1, 2], F32))
        thrp = E(nc.sbuf_tensor("thrp", [1, 4], F32))
        thrc1 = E(nc.sbuf_tensor("thrc1", [128, 2], F32))
        thrc2 = E(nc.sbuf_tensor("thrc2", [128, 2], F32))
        E12 = [E(nc.sbuf_tensor(f"E12_{i}", [128, 512], BF16)) for i in range(2)]
        bnst = E(nc.sbuf_tensor("bnst", [128, 24], F32))
        bnagg = E(nc.sbuf_tensor("bnagg", [128, 4], F32))
        sdcol = E(nc.sbuf_tensor("sdcol", [128, 2], F32))
        rlncol = E(nc.sbuf_tensor("rlncol", [128, 2], F32))
        absp = E(nc.sbuf_tensor("absp", [128, 1], F32))
        epsc = E(nc.sbuf_tensor("epsc", [128, 2], F32))
        XT = E(nc.sbuf_tensor("XT", [128, 16384], BF16))
        KT = E(nc.sbuf_tensor("KT", [128, 16384], BF16))
        stage = [E(nc.sbuf_tensor(f"stage_{i}", [128, 1024], F32)) for i in range(2)]
        wbuf_cm = nc.sbuf_tensor("WBUF", [128, 8192], BF16, side="right")
        WBUF = wbuf_cm.__enter__()
        qt_cm = nc.sbuf_tensor("QT", [128, 16384], BF16)
        QT = qt_cm.__enter__()
        vh_cm = nc.sbuf_tensor("VH", [128, 24576], BF16)
        VH = vh_cm.__enter__()
        row_cm = [nc.sbuf_tensor(n, sh, F32) for n, sh in
                  (("bvrow_sb", [1, DM]), ("borow_sb", [1, DM]),
                   ("b2row_sb", [1, DM]), ("bvT0row", [1, DM]),
                   ("bvVrow", [1, 1536]))]
        bvrow, borow, b2row, bvT0row, bvVrow = [cm.__enter__() for cm in row_cm]

        PB = [E(nc.psum_tensor(f"pb{i}", [128, 512], F32)) for i in range(8)]

        sems = {s: E(nc.semaphore(f"sem_{s}"))
                for s in ("T", "S", "V", "G", "DSU", "DSV", "DVH", "DWV",
                          "DWO", "DS0", "DS1", "DX0", "DX1", "DO0", "DO1",
                          "DAB0", "DAB1", "DAB2", "DAB3",
                          "DAB4", "DAB5", "DAB6", "DAB7")}

        rPB = [Reg() for _ in range(8)]
        rQT = Reg()
        rKT = Reg()
        rStage = [Reg(), Reg()]
        rE = [Reg() for _ in range(4)]
        rE12 = [Reg(), Reg()]
        rStk = Reg()
        rRinvb = [Reg(), Reg()]
        rAB4 = [Reg(), Reg()]
        rAbsp = Reg()
        rRf = [Reg(), Reg()]
        rRb = [Reg(), Reg()]
        rSmall = Reg()
        rLate = Reg()
        rGam = Reg()
        rThr = Reg()
        rAcc1 = Reg()
        rAcc2 = Reg()
        rGtmp = Reg()
        rBnP = [Reg(), Reg()]

        def vop(fn, reads=(), writes=()):
            return sch.op(V, fn, reads, writes)

        def sop(fn, reads=(), writes=()):
            return sch.op(SC, fn, reads, writes)

        def top(fn, reads=(), writes=()):
            return sch.op(T, fn, reads, writes)

        def ld(dst, src, q="sync"):
            sch.dma(q, lambda e, d=dst, s=src: e.dma_start(d, s),
                    "DSU" if q == "scalar" else "DSV",
                    writes=[rSmall if q == "scalar" else rLate])

        ld(identf[:, :], d_idf[:, :], "scalar")
        ld(bq8[:, :], d_bq[:, :], "scalar")
        ld(bk8[:, :], d_bk[:, :], "scalar")
        ld(relks[0:5, 0:64], d_rk[:, :], "scalar")
        ld(relvs[0:5, 0:64], d_rv[:, :], "scalar")
        ld(b1s[:, :], d_b1[:, :], "scalar")

        rXTk = [Reg() for _ in range(ND)]
        rWBs = [Reg() for _ in range(ND)]
        rWkk = [Reg() for _ in range(ND)]
        for k in range(ND):
            thr = [rXTk[k - 3]] if k >= 3 else []
            sch.dma(G, lambda e, k=k: e.dma_start(
                WBUF[:, 1024 * k:1024 * k + 1024],
                d_wq[128 * k:128 * k + 128, :]), f"DAB{k}",
                reads=list(thr), writes=[rWBs[k]])
            tok = sch.dma(G, lambda e, k=k: e.dma_start(
                XT[:, 2048 * k:2048 * k + 2048],
                d_xT[128 * k:128 * k + 128, :]), f"DAB{k}",
                reads=list(thr), writes=[rXTk[k]])
            rWBs[k].w = tok
        ld(idd129[:, :], d_idb[:, :])
        ld(identu[:, :], d_idu[:, :])
        ld(m0c[:, :], d_tng[:, :])
        ld(onesxl[:, :], d_oxl[:, :])
        ld(elast[:, :], d_els[:, :])
        ld(o1f[:, :], d_o1f[:, :])
        ld(o1b[:, :], d_o1b[:, :])
        ld(oall[:, :], d_oall[:, :])
        ld(inv128[:, :], d_i128[:, :])
        ld(bvrow[:, :], d_bv[:, :])
        ld(borow[:, :], d_bo[:, :])
        ld(b2row[:, :], d_b2[:, :])

        for k in range(ND):
            tok = sch.dma(G, lambda e, k=k: e.dma_start(
                VH[:, 8192 + 1024 * k:8192 + 1024 * k + 1024],
                d_wk[128 * k:128 * k + 128, :]), "DVH",
                reads=[rXTk[7]] if k == 0 else [], writes=[rWkk[k]])
        for k in range(ND):
            rWkk[k].w = tok

        for e12, r in zip(E12, rE12):
            vop(lambda e, tt=e12: e.memset(tt[:, :], 0.0), writes=[r])
        vop(lambda e: e.memset(onescol[:, :], 1.0), writes=[rSmall])
        vop(lambda e: e.memset(PB[5][:, 0:8], 0.0), writes=[rPB[5]])
        vop(lambda e: e.memset(PB[4][0:34, :], 0.0), writes=[rPB[4]])
        vop(lambda e: e.memset(epsc[:, 0:1], EPS), writes=[rSmall])
        vop(lambda e: e.memset(epsc[:, 1:2], 0.5 * EPS), writes=[rSmall])
        vop(lambda e: e.memset(accW1[:, :], 0.0), writes=[rAcc1])
        vop(lambda e: e.memset(accW2[:, :], 0.0), writes=[rAcc2])
        for selt in (sel1E, sel1O, sel2E, sel2O):
            vop(lambda e, tt=selt: e.memset(tt[:, :], 0.0), writes=[rSmall])
        vop(lambda e: e.memset(sel1E[0:64, 0:1], 1.0), writes=[rSmall])
        vop(lambda e: e.memset(sel1O[64:128, 0:1], 1.0), writes=[rSmall])
        vop(lambda e: e.memset(sel2E[0:64, 1:2], 1.0), writes=[rSmall])
        vop(lambda e: e.memset(sel2O[64:128, 1:2], 1.0), writes=[rSmall])
        vop(lambda e: e.tensor_copy(relks[0:5, 64:128], relks[0:5, 0:64]),
            reads=[rSmall], writes=[rSmall])
        vop(lambda e: e.tensor_copy(relvs[0:5, 64:128], relvs[0:5, 0:64]),
            reads=[rSmall], writes=[rSmall])
        top(lambda e: e.transpose(PB[7][0:128, 0:5], relks[0:5, :],
                                  identf[0:5, 0:5]),
            reads=[rSmall], writes=[rPB[7]])
        sop(lambda e: e.copy(relkT[:, :], PB[7][0:128, 0:5]),
            reads=[rPB[7]], writes=[rSmall])
        top(lambda e: e.transpose(PB[7][0:128, 0:5], relvs[0:5, :],
                                  identf[0:5, 0:5]),
            reads=[rSmall], writes=[rPB[7]])
        sop(lambda e: e.copy(relvT[:, :], PB[7][0:128, 0:5]),
            reads=[rPB[7]], writes=[rSmall])
        for src8, dst8 in ((bq8, bqcols), (bk8, bkpcols)):
            top(lambda e, s8=src8: e.transpose(PB[7][0:128, 0:8], s8[:, :],
                                               identf[0:8, 0:8]),
                reads=[rSmall], writes=[rPB[7]])
            sop(lambda e, d8=dst8: e.copy(d8[:, :], PB[7][0:128, 0:8]),
                reads=[rPB[7]], writes=[rSmall])
        top(lambda e: e.transpose(PB[7][0:128, 0:32], b1s[:, :],
                                  identf[0:32, 0:32]),
            reads=[rSmall], writes=[rPB[7]])
        sop(lambda e: e.copy(b1cols[:, :], PB[7][0:128, 0:32]),
            reads=[rPB[7]], writes=[rSmall])
        vop(lambda e: e.tensor_scalar(bkpcols[:, :], bkpcols[:, :],
                                      relkT[:, 0:1], None, ALU.add),
            reads=[rSmall], writes=[rSmall])
        for tz, lo, hi in ((TzE, 0, 64), (TzO, 64, 128)):
            vop(lambda e, tt=tz: e.memset(tt[:, :], 0.0), writes=[rSmall])
            for col, src in ((0, 1), (1, 2)):
                vop(lambda e, tt=tz, a=lo, b=hi, c=col, s=src: e.tensor_tensor(
                    tt[a:b, c:c + 1], relkT[a:b, s:s + 1], relkT[a:b, 0:1],
                    ALU.subtract), reads=[rSmall], writes=[rSmall])
        for tv, lo, hi in ((TvBmE, 0, 64), (TvBmO, 64, 128)):
            vop(lambda e, tt=tv: e.memset(tt[:, :], 0.0), writes=[rSmall])
            for col, src in ((0, 1), (1, 2)):
                vop(lambda e, tt=tv, a=lo, b=hi, c=col, s=src: e.tensor_tensor(
                    tt[a:b, c:c + 1], relvT[a:b, s:s + 1], relvT[a:b, 0:1],
                    ALU.subtract), reads=[rSmall], writes=[rSmall])
        vop(lambda e: e.memset(Tv2zE[:, :], 0.0), writes=[rSmall])
        top(lambda e: e.transpose(PB[7][0:2, 0:128], TvBmE[:, :], identf[:, :]),
            reads=[rSmall], writes=[rPB[7]])
        sop(lambda e: e.copy(Tv2zE[0:2, :], PB[7][0:2, 0:128]),
            reads=[rPB[7]], writes=[rSmall])
        vop(lambda e: e.memset(Tv2zO[:, :], 0.0), writes=[rSmall])
        top(lambda e: e.transpose(PB[7][0:2, 0:128], TvBmO[:, :], identf[:, :]),
            reads=[rSmall], writes=[rPB[7]])
        sop(lambda e: e.copy(Tv2zO[0:2, :], PB[7][0:2, 0:128]),
            reads=[rPB[7]], writes=[rSmall])
        for dst4, eh, oh in ((sel41, sel1E, sel1O), (sel42, sel2E, sel2O),
                             (Tz4, TzE, TzO)):
            for c_, (s_, k_) in enumerate(((eh, 0), (oh, 0), (eh, 1),
                                           (oh, 1))):
                vop(lambda e, d4=dst4, s=s_, k=k_, c=c_: e.tensor_copy(
                    d4[:, c:c + 1], s[:, k:k + 1]),
                    reads=[rSmall], writes=[rSmall])
        for tv4, tvm, ca, cb in ((Tv2z4E, TvBmE, 0, 2), (Tv2z4O, TvBmO, 1, 3)):
            vop(lambda e, t4=tv4: e.memset(t4[:, :], 0.0), writes=[rSmall])
            vop(lambda e: e.memset(TvBm4[:, :], 0.0), writes=[rSmall])
            vop(lambda e, tm=tvm, c=ca: e.tensor_copy(
                TvBm4[:, c:c + 1], tm[:, 0:1]), reads=[rSmall],
                writes=[rSmall])
            vop(lambda e, tm=tvm, c=cb: e.tensor_copy(
                TvBm4[:, c:c + 1], tm[:, 1:2]), reads=[rSmall],
                writes=[rSmall])
            top(lambda e: e.transpose(PB[7][0:4, 0:128], TvBm4[:, :],
                                      identf[:, :]),
                reads=[rSmall], writes=[rPB[7]])
            sop(lambda e, t4=tv4: e.copy(t4[0:4, :], PB[7][0:4, 0:128]),
                reads=[rPB[7]], writes=[rSmall])
        for i in range(16):
            vop(lambda e, i=i: e.tensor_tensor(
                bvT0row[0:1, 64 * i:64 * i + 64],
                bvrow[0:1, 64 * i:64 * i + 64],
                relvs[0:1, 0:64], ALU.add), reads=[rSmall, rLate],
                writes=[rSmall])
        vop(lambda e: e.memset(bvVrow[0:1, :], 0.0), writes=[rSmall])
        for pp in range(8):
            vop(lambda e, pp=pp: e.tensor_copy(
                bvVrow[0:1, 192 * pp:192 * pp + 64],
                bvT0row[0:1, 128 * pp:128 * pp + 64]),
                reads=[rSmall], writes=[rSmall])
            vop(lambda e, pp=pp: e.tensor_copy(
                bvVrow[0:1, 192 * pp + 128:192 * pp + 192],
                bvT0row[0:1, 128 * pp + 64:128 * pp + 128]),
                reads=[rSmall], writes=[rSmall])
        for c in range(3):
            top(lambda e, c=c: e.matmul(PB[6][:, :], o1f[0:1, :],
                                        bvVrow[0:1, 512 * c:512 * c + 512],
                                        start=True, stop=True),
                reads=[rSmall, rLate], writes=[rPB[6]])
            sop(lambda e, c=c: e.copy(bvVb[:, 512 * c:512 * c + 512], PB[6][:, :]),
                reads=[rPB[6]], writes=[rSmall])
        for row, dstb in ((borow, bob), (b2row, b2b)):
            for c in range(2):
                top(lambda e, r=row, c=c: e.matmul(
                    PB[6][:, :], o1f[0:1, :], r[0:1, 512 * c:512 * c + 512],
                    start=True, stop=True),
                    reads=[rSmall, rLate], writes=[rPB[6]])
                sop(lambda e, d=dstb, c=c: e.copy(d[:, 512 * c:512 * c + 512],
                                                  PB[6][:, :]),
                    reads=[rPB[6]], writes=[rSmall])

        for cm in row_cm[::-1]:
            cm.__exit__(None, None, None)
        ep_cm = [nc.sbuf_tensor(f"Ep_{i}", [128, 1024], BF16) for i in range(4)]
        Ep = [cm.__enter__() for cm in ep_cm]
        stk_cms = [nc.sbuf_tensor(f"stk{i}", [128, 512], BF16)
                   for i in range(4)]
        stk1, stk2, stk3, stk4 = [cm.__enter__() for cm in stk_cms]
        rinvb_cm = [nc.sbuf_tensor(f"rinvb_{i}", [128, 512], F32) for i in range(2)]
        rinvb = [cm.__enter__() for cm in rinvb_cm]
        ab_cm = [nc.sbuf_tensor(f"ab4_{i}", [128, 4], F32) for i in range(2)]
        AB4 = [cm.__enter__() for cm in ab_cm]
        rft_cm = nc.sbuf_tensor("rft", [128, 1024], F32)
        rft = rft_cm.__enter__()
        eab_cm = nc.sbuf_tensor("eab8", [128, 8], BF16)
        eab = eab_cm.__enter__()
        cw_cm = [nc.sbuf_tensor("cwB", [128, 258], BF16)]
        cwB = cw_cm[0].__enter__()
        cw = [cwB[:, 0:129], cwB[:, 129:258]]
        rbt1_cm = nc.sbuf_tensor("rbt1", [128, 512], BF16)
        rbt1 = rbt1_cm.__enter__()

        def qk_proj(wregs, wb, wofs, biascols, dstT, rdst):
            for m in range(8):
                for n in range(4):
                    pb = (m * 4 + n) % 4
                    for k in range(ND):
                        first, last = (k == 0), (k == ND - 1)
                        sch.op(T, lambda e, pb=pb, k=k, m=m, n=n, wb=wb, wo=wofs,
                               f=first, l=last: e.matmul(
                                   PB[pb][:, :],
                                   wb[:, wo + 1024 * k + 128 * m:
                                      wo + 1024 * k + 128 * m + 128],
                                   XT[:, 2048 * k + 512 * n:
                                      2048 * k + 512 * n + 512],
                                   start=f, stop=l),
                               reads=[wregs[k], rXTk[k]],
                               writes=([rPB[pb]] if (first or last) else []))
                    sch.op(SC, lambda e, pb=pb, m=m, n=n, bc=biascols, dt=dstT:
                           e.activation(dt[:, 2048 * m + 512 * n:
                                           2048 * m + 512 * n + 512],
                                        PB[pb][:, :], AF.Identity,
                                        bias=bc[:, m:m + 1], scale=1.0),
                           reads=[rPB[pb], rSmall], writes=[rdst])

        qk_proj(rWBs, WBUF, 0, bqcols, QT, rQT)
        sch.dma(G, lambda e: e.dma_start(
            WBUF[:, :].rearrange("p (k n) -> p k n", k=8),
            d_wv.rearrange("(k p) n -> p k n", p=128)), "DWV",
            writes=list(rWBs))
        qk_proj(rWkk, VH, 8192, bkpcols, KT, rKT)
        depWk = []
        for rr_ in rWkk:
            depWk += rr_.tokens()
        rVHs = [Reg(depWk) for _ in range(NT)]
        vop(lambda e: e.memset(
            VH[:, :].rearrange("p (g c) -> p g c", c=192)[:, :, 64:65], 1.0),
            writes=list(rVHs))
        vop(lambda e: e.memset(
            VH[:, :].rearrange("p (g c) -> p g c", c=192)[:, :, 65:128], 0.0),
            writes=list(rVHs))
        for s in range(NT):
            for dd in range(2):
                pb = (s * 2 + dd) % 4
                for k in range(ND):
                    first, last = (k == 0), (k == ND - 1)
                    sch.op(T, lambda e, pb=pb, k=k, s=s, dd=dd, f=first, l=last:
                           e.matmul(
                               PB[pb][:, :],
                               XT[:, 2048 * k + 128 * s:2048 * k + 128 * s + 128],
                               WBUF[:, 1024 * k + 512 * dd:
                                    1024 * k + 512 * dd + 512],
                               start=f, stop=l),
                           reads=[rWBs[k], rXTk[k]],
                           writes=([rPB[pb]] if (first or last) else []))
                base = 1536 * s + 768 * dd
                bvof = 768 * dd
                for hh in range(2):
                    vop(lambda e, pb=pb, b=base, bo_=bvof, hh=hh:
                        e.scalar_tensor_tensor(
                            VH[:, b:b + 768].rearrange(
                                "p (g c) -> p g c", g=4)[:, :, 128 * hh:
                                                         128 * hh + 64],
                            PB[pb][:, :].rearrange(
                                "p (g c) -> p g c", g=4)[:, :, 64 * hh:
                                                         64 * hh + 64],
                            1.0,
                            bvVb[:, bo_:bo_ + 768].rearrange(
                                "p (g c) -> p g c", g=4)[:, :, 128 * hh:
                                                         128 * hh + 64],
                            ALU.mult, ALU.add),
                        reads=[rPB[pb], rSmall, rVHs[s]], writes=[rVHs[s]])
        sch.dma(G, lambda e: e.dma_start(
            WBUF[:, :].rearrange("p (k n) -> p k n", k=8),
            d_wo.rearrange("(k p) n -> p k n", p=128)), "DWO",
            writes=list(rWBs))

        depXT = []
        for rr_ in rXTk:
            depXT += rr_.tokens()
        rAOT = [[Reg(depXT) for _ in range(NT)] for _ in range(ND)]

        bg = []

        def gamma_unit(i, src, wcols, acc, racc):
            st = i % 2

            def emit():
                sch.dma(G, lambda e, s=src, w=wcols, st=st:
                        e.dma_start(stage[st][:, 0:w], s), f"DS{st}",
                        writes=[rStage[st]])
                vop(lambda e, w=wcols, st=st, i=i: e.tensor_reduce(
                    accWall[:, i:i + 1], stage[st][:, 0:w], AX.X, ALU.add,
                    apply_absolute_value=True),
                    reads=[rStage[st]], writes=[racc])
            return emit

        gi = 0
        for k in range(ND):
            for h in range(4):
                bg.append(gamma_unit(gi, d_w1t[128 * k:128 * k + 128,
                                               1024 * h:1024 * h + 1024],
                                     1024, accW1, rAcc1))
                gi += 1
        for j in range(NF):
            bg.append(gamma_unit(gi, d_w2t[128 * j:128 * j + 128, :], 1024,
                                 accW2, rAcc2))
            gi += 1

        def gamma_finish(acc, racc, thrc, gcol):
            vop(lambda e, a=acc, g=gcol: e.tensor_reduce(
                a[:, :], accWall[:, 32 * g:32 * g + 32], AX.X, ALU.add),
                reads=[racc], writes=[racc])
            top(lambda e, a=acc: e.matmul(PB[6][0:1, 0:1], a[:, :],
                                          onescol[:, :], start=True,
                                          stop=True),
                reads=[racc, rSmall], writes=[rPB[6]])
            sop(lambda e, g=gcol: e.activation(
                gam[0:1, g:g + 1], PB[6][0:1, 0:1], AF.Identity,
                scale=1.0 / (FF * DM)), reads=[rPB[6]], writes=[rGam])
            sop(lambda e, g=gcol: e.activation(
                thrp[0:1, 2 * g:2 * g + 1], gam[0:1, g:g + 1], AF.Identity,
                scale=0.5, bias=epsc[0:1, 1:2]),
                reads=[rGam, rSmall], writes=[rThr])
            sop(lambda e, g=gcol: e.activation(
                thrp[0:1, 2 * g + 1:2 * g + 2], thrp[0:1, 2 * g:2 * g + 1],
                AF.Identity, scale=-1.0), reads=[rThr], writes=[rThr])
            top(lambda e, g=gcol: e.matmul(PB[6][0:128, 0:2], o1f[0:1, :],
                                           thrp[0:1, 2 * g:2 * g + 2],
                                           start=True, stop=True),
                reads=[rThr, rSmall, rLate], writes=[rPB[6]])
            sop(lambda e, tc=thrc: e.copy(tc[:, :], PB[6][0:128, 0:2]),
                reads=[rPB[6]], writes=[rThr])

        bg.append(lambda: gamma_finish(accW1, rAcc1, thrc1, 0))
        bg.append(lambda: gamma_finish(accW2, rAcc2, thrc2, 1))
        bgi = 0

        def drain_bg(n):
            nonlocal bgi
            while n > 0 and bgi < len(bg):
                bg[bgi]()
                bgi += 1
                n -= 1
        PBO = [0, 1]
        PSS = [2, 3, 7]

        rEabC = Reg()
        rCwB = Reg()
        rStkS = [Reg(), Reg()]
        HBv = [0, 64]
        SEL1v = [sel1E, sel1O]
        SEL2v = [sel2E, sel2O]
        TZv = [TzE, TzO]
        TV2Zv = [Tv2zE, Tv2zO]
        STKP = [(stk1, stk2), (stk3, stk4)]

        def emit_stks(pg, sl=0):
            t, pair = divmod(pg, 8)
            q0 = 512 * t
            qof = 2048 * pair
            sA, sB = STKP[sl]
            rsl = rStkS[sl]
            if t == 0:
                sch.op(G, lambda e, sA=sA: e.memset(sA[:, 0:1], 0.0),
                       writes=[rsl])
                sch.op(G, lambda e, qo=qof, sA=sA: e.tensor_tensor(
                    sA[:, 1:512], QT[:, qo + 1:qo + 512],
                    KT[:, qo + 0:qo + 511], ALU.mult),
                    reads=[rQT, rKT], writes=[rsl])
            else:
                sch.op(G, lambda e, qo=qof, q0=q0, sA=sA: e.tensor_tensor(
                    sA[:, :], QT[:, qo + q0:qo + q0 + 512],
                    KT[:, qo + q0 - 1:qo + q0 + 511], ALU.mult),
                    reads=[rQT, rKT], writes=[rsl])
            sch.op(G, lambda e, qo=qof, q0=q0, sB=sB: e.tensor_tensor(
                sB[:, :], QT[:, qo + q0:qo + q0 + 512],
                KT[:, qo + q0:qo + q0 + 512], ALU.mult),
                reads=[rQT, rKT], writes=[rsl])

        def emit_tail_pre(pg, hh):
            t, pair = divmod(pg, 8)
            pO = PBO[hh]
            rr = 64 if hh == 0 else 0
            sop(lambda e, rr=rr, pO=pO: e.activation(
                rft[rr:rr + 1, 512:1024], PB[pO][rr:rr + 1, :], AF.Ln),
                reads=[rPB[pO]], writes=[rRf[0]])
            sop(lambda e, rr=rr: e.activation(
                rbt1[rr:rr + 1, :], rft[rr:rr + 1, 512:1024], AF.Exp,
                scale=-1.0),
                reads=[rRf[0]], writes=[rRb[0]])

        def emit_tail_post(pg, hh):
            t, pair = divmod(pg, 8)
            q0 = 512 * t
            qof = 2048 * pair
            pO = PBO[hh]
            rr = 64 if hh == 0 else 0
            av0 = 0 if hh == 0 else 64
            rrvb = rRinvb[hh]
            rvb = rinvb[hh]
            top(lambda e, rr=rr: e.matmul(PB[6][:, :], oall[rr:rr + 1, 0:128],
                                          rbt1[rr:rr + 1, :],
                                          start=True, stop=True),
                reads=[rRb[0], rSmall, rLate], writes=[rPB[6]])
            vop(lambda e, rv=rvb: e.tensor_copy(rv[:, :], PB[6][:, :]),
                reads=[rPB[6]], writes=[rrvb])
            aotregs = [rAOT[pair][4 * t + i] for i in range(4)]
            vop(lambda e, av0=av0, qo=qof, q0=q0, rv=rvb, pO=pO:
                e.tensor_tensor(
                    XT[av0:av0 + 64, qo + q0:qo + q0 + 512],
                    PB[pO][av0:av0 + 64, :], rv[av0:av0 + 64, :],
                    ALU.mult),
                reads=[rPB[pO], rrvb], writes=aotregs)

        pending = []
        PGSEQ = []
        for p_ in range(8):
            PGSEQ += [24 + p_, p_]
        for p_ in range(8):
            PGSEQ += [16 + p_, 8 + p_]
        emit_stks(PGSEQ[0])
        for ui in range(4 * NPAIR):
            pg = PGSEQ[ui]
            t, pair = divmod(pg, 8)
            q0 = 512 * t
            nkt = 4 * t + 4
            qof = 2048 * pair
            sl = ui % 2
            sA, sB = STKP[sl]
            rsl = rStkS[sl]
            drain_bg(3)
            PO = [PBO[hh] for hh in range(2)]
            if ui + 1 < 4 * NPAIR:
                emit_stks(PGSEQ[ui + 1], (ui + 1) % 2)
            for f in pending:
                f[0]()

            top(lambda e, sA=sA: e.matmul(
                PB[4][0:4, :], sel41[:, :], sA[:, :],
                start=True, stop=False),
                reads=[rsl], writes=[rPB[4]])
            top(lambda e, sB=sB: e.matmul(
                PB[4][0:4, :], sel42[:, :], sB[:, :],
                start=False, stop=False),
                reads=[rsl], writes=[])
            top(lambda e, qo=qof, q0=q0: e.matmul(
                PB[4][0:4, :], Tz4[:, :], QT[:, qo + q0:qo + q0 + 512],
                start=False, stop=True),
                reads=[rQT, rSmall], writes=[rPB[4]])
            sop(lambda e: e.activation(
                E12[0][0:4, :], PB[4][0:4, :], AF.Exp, scale=SCALE),
                reads=[rPB[4]], writes=[rE12[0]])
            if t == 0:
                vop(lambda e: e.memset(E12[0][0:2, 0:1], 0.0),
                    writes=[rE12[0]])

            def emit_av(hh, kt):
                j = kt - 4 * t
                c0 = 128 * j if j > 0 else 0
                vofs = 1536 * kt + 192 * pair + 64 * hh
                first = (kt == 0)
                pO = PO[hh]
                ei = kt % 4
                sch.op(T, lambda e, c0=c0, vo=vofs, f=first, pO=pO, ei=ei,
                       hh=hh: e.matmul(
                           PB[pO][:, c0:512],
                           VH[:, vo:vo + 128],
                           Ep[ei][:, 512 * hh + c0:512 * hh + 512],
                           start=f, stop=False),
                       reads=[rVHs[kt], rE[ei]],
                       writes=([rPB[pO]] if first else []))

            for kt in range(nkt):
                j = kt - 4 * t
                c0 = 128 * j if j > 0 else 0
                kof = qof + 128 * kt
                spill = (t > 0 and kt == 4 * t - 1)
                ei = kt % 4
                for hh in range(2):
                    hb = HBv[hh]
                    pS = PSS[(2 * kt + hh) % 3]
                    sch.op(T, lambda e, kof=kof, c0=c0, hb=hb, hh=hh, pS=pS,
                           qo=qof, q0=q0: e.matmul(
                               PB[pS][:, c0:512],
                               KT[hb:hb + 64, kof:kof + 128],
                               QT[hb:hb + 64, qo + q0 + c0:qo + q0 + 512],
                               start=True, stop=True),
                           reads=[rKT, rQT], writes=[rPB[pS]])
                if spill or j >= 0:
                    w0 = qof + q0 + c0 if j >= 0 else None
                    for hh in range(2):
                        hb = HBv[hh]
                        if spill:
                            top(lambda e, hb=hb, qo=qof, q0=q0, tz=TZv[hh],
                                hh=hh: e.matmul(
                                    PB[5][:, 4 * hh:4 * hh + 2],
                                    QT[hb:hb + 64,
                                       qo + q0 - 127:qo + q0 + 1],
                                    tz[hb:hb + 64, :],
                                    start=True, stop=True),
                                reads=[rQT, rSmall], writes=[rPB[5]])
                        else:
                            top(lambda e, hb=hb, w0=w0, tz=TZv[hh], hh=hh:
                                e.matmul(
                                    PB[5][:, 4 * hh:4 * hh + 2],
                                    QT[hb:hb + 64, w0:w0 + 128],
                                    tz[hb:hb + 64, :],
                                    start=True, stop=True),
                                reads=[rQT, rSmall], writes=[rPB[5]])
                            wid = 127 if j == 3 else 128
                            top(lambda e, hb=hb, w0=w0, tz=TZv[hh], wd=wid,
                                hh=hh: e.matmul(
                                    PB[5][0:wd, 4 * hh + 2:4 * hh + 4],
                                    QT[hb:hb + 64, w0 + 1:w0 + 1 + wd],
                                    tz[hb:hb + 64, :],
                                    start=True, stop=True),
                                reads=[rQT, rSmall], writes=[rPB[5]])
                    sop(lambda e: e.activation(eab[:, :], PB[5][:, 0:8],
                                               AF.Exp, scale=SCALE),
                        reads=[rPB[5]], writes=[rEabC])
                for hh in range(2):
                    pS = PSS[(2 * kt + hh) % 3]
                    sop(lambda e, ei=ei, hh=hh, c0=c0, pS=pS: e.activation(
                        Ep[ei][:, 512 * hh + c0:512 * hh + 512],
                        PB[pS][:, c0:512], AF.Exp, scale=SCALE),
                        reads=[rPB[pS]], writes=[rE[ei]])
                if j >= 0:
                    wde = 128 if j == 3 else 129
                    for hh in range(2):
                        vop(lambda e, hh=hh, wde=wde: e.scalar_tensor_tensor(
                            cw[hh][:, 0:wde], idd129[:, 0:wde],
                            eab[:, 4 * hh + 1:4 * hh + 2], m0c[:, 0:wde],
                            ALU.mult, ALU.add),
                            reads=[rEabC, rSmall, rLate], writes=[rCwB])
                        vop(lambda e, hh=hh, wde=wde: e.scalar_tensor_tensor(
                            cw[hh][:, 0:wde], identu[:, 0:wde],
                            eab[:, 4 * hh + 2:4 * hh + 3], cw[hh][:, 0:wde],
                            ALU.mult, ALU.add),
                            reads=[rEabC, rSmall, rLate], writes=[rCwB])
                    vop(lambda e, ei=ei, c0=c0, wde=wde: e.tensor_tensor(
                        Ep[ei][:, :].rearrange(
                            "p (g c) -> p g c", g=2)[:, :, c0:c0 + wde],
                        Ep[ei][:, :].rearrange(
                            "p (g c) -> p g c", g=2)[:, :, c0:c0 + wde],
                        cwB[:, :].rearrange(
                            "p (g c) -> p g c", g=2)[:, :, 0:wde],
                        ALU.mult),
                        reads=[rCwB], writes=[rE[ei]])
                if spill:
                    for hh in range(2):
                        vop(lambda e, hh=hh: e.scalar_tensor_tensor(
                            cw[hh][:, 0:1], elast[:, :],
                            eab[:, 4 * hh:4 * hh + 1], onesxl[:, :],
                            ALU.mult, ALU.add),
                            reads=[rEabC, rSmall, rLate], writes=[rCwB])
                    vop(lambda e, ei=ei: e.tensor_tensor(
                        Ep[ei][:, :].rearrange(
                            "p (g c) -> p g c", g=2)[:, :, 0:1],
                        Ep[ei][:, :].rearrange(
                            "p (g c) -> p g c", g=2)[:, :, 0:1],
                        cwB[:, :].rearrange(
                            "p (g c) -> p g c", g=2)[:, :, 0:1],
                        ALU.mult),
                        reads=[rCwB], writes=[rE[ei]])
                if kt == 1:
                    while pending:
                        pending.pop(0)[1]()
                if kt > 2:
                    emit_av(0, kt - 3)
                    emit_av(1, kt - 3)
            for kk in (nkt - 3, nkt - 2, nkt - 1):
                emit_av(0, kk)
                emit_av(1, kk)
            TV4 = [Tv2z4E, Tv2z4O]
            for hh in range(2):
                top(lambda e, tv=TV4[hh], pO=PO[hh]: e.matmul(
                    PB[pO][:, :], tv[:, :], E12[0][:, :], start=False,
                    stop=True),
                    reads=[rSmall, rE12[0]], writes=[rPB[PO[hh]]])
                pending.append((lambda pg=pg, hh=hh: emit_tail_pre(pg, hh),
                                lambda pg=pg, hh=hh: emit_tail_post(pg, hh)))
        for f in pending:
            f[0]()
        while pending:
            pending.pop(0)[1]()

        depQT = rQT.tokens()
        depVH = list(depWk)
        for rr_ in rVHs:
            depVH += rr_.tokens()
        depKT = rKT.tokens()
        depB = list(depQT) + list(depVH)
        for rr_ in (rE + [rStk] + rRinvb + rAB4 + rRf + rRb + [rAbsp]):
            depB += rr_.tokens()
        rbt1_cm.__exit__(None, None, None)
        for c in cw_cm[::-1]:
            c.__exit__(None, None, None)
        eab_cm.__exit__(None, None, None)
        rft_cm.__exit__(None, None, None)
        for cm in ab_cm[::-1]:
            cm.__exit__(None, None, None)
        for cm in rinvb_cm[::-1]:
            cm.__exit__(None, None, None)
        for cm in stk_cms[::-1]:
            cm.__exit__(None, None, None)
        for cm in ep_cm[::-1]:
            cm.__exit__(None, None, None)
        vh_cm.__exit__(None, None, None)
        qt_cm.__exit__(None, None, None)

        W1F8 = E(nc.sbuf_tensor("W1F8", [128, 32768], FP8))
        W2F8 = E(nc.sbuf_tensor("W2F8", [128, 32768], FP8))
        zbuf = [E(nc.sbuf_tensor(f"zbuf_{i}", [128, DM], F32)) for i in range(2)]
        xres = [E(nc.sbuf_tensor(f"xres_{i}", [128, DM], F32)) for i in range(2)]
        pscr = [E(nc.sbuf_tensor(f"pscr_{i}", [128, 1024], BF16))
                for i in range(2)]
        rW1 = [[Reg(depB) for _ in range(4)] for _ in range(ND)]
        rW2 = [Reg(depB) for _ in range(NF)]
        rX1N = [Reg(depKT) for _ in range(NT)]
        rZ = [Reg(depB), Reg(depB)]
        rXres = [Reg(depB), Reg(depB)]
        rPscr = Reg(depB)

        w1units = []
        for hf in range(4):
            for k in range(ND):
                w1units.append(("sign",
                                d_w1t[128 * k:128 * k + 128,
                                      1024 * hf:1024 * hf + 1024],
                                1024, W1F8, 4096 * k + 1024 * hf, thrc1,
                                rW1[k][hf]))
        w2units = [("dve", d_w2t[128 * j:128 * j + 128, :], 1024,
                    W2F8, 1024 * j, thrc2, rW2[j]) for j in range(NF)]
        qorder = w1units + w2units
        qunits = [(qi % 2,) + u for qi, u in enumerate(qorder)]
        qidx = 0

        def quant_unit(st, kind, src, wcols, dst, dcol, thrc, rdst):
            sch.dma(G, lambda e, s=src, w=wcols, st=st:
                    e.dma_start(stage[st][:, 0:w], s), f"DS{st}",
                    writes=[rStage[st]])
            if kind == "sign":
                sop(lambda e, w=wcols, st=st, tc=thrc: e.activation(
                    pscr[0][:, 0:w], stage[st][:, 0:w], AF.Sign,
                    bias=tc[:, 1:2], scale=1.0),
                    reads=[rStage[st], rThr], writes=[rPscr])
                sop(lambda e, w=wcols, st=st, tc=thrc: e.activation(
                    pscr[1][:, 0:w], stage[st][:, 0:w], AF.Sign,
                    bias=tc[:, 0:1], scale=1.0),
                    reads=[rStage[st], rThr], writes=[rPscr])
                vop(lambda e, w=wcols, d=dst, c=dcol: e.tensor_tensor(
                    d[:, c:c + w], pscr[0][:, 0:w], pscr[1][:, 0:w],
                    ALU.add),
                    reads=[rPscr], writes=[rdst])
            else:
                vop(lambda e, w=wcols, st=st, tc=thrc: e.tensor_scalar(
                    pscr[0][:, 0:w], stage[st][:, 0:w], tc[:, 0:1], None,
                    ALU.is_ge),
                    reads=[rStage[st], rThr], writes=[rPscr])
                vop(lambda e, w=wcols, st=st, tc=thrc: e.tensor_scalar(
                    pscr[1][:, 0:w], stage[st][:, 0:w], tc[:, 1:2], None,
                    ALU.is_le),
                    reads=[rStage[st], rThr], writes=[rPscr])
                vop(lambda e, w=wcols, d=dst, c=dcol: e.tensor_tensor(
                    d[:, c:c + w], pscr[0][:, 0:w], pscr[1][:, 0:w],
                    ALU.subtract),
                    reads=[rPscr], writes=[rdst])

        def drain_quant(n):
            nonlocal qidx
            while n > 0 and qidx < len(qunits):
                quant_unit(*qunits[qidx])
                qidx += 1
                n -= 1

        sch.dma(G, lambda e: e.dma_start(xres[0][:, :], d_x[0:128, :]),
                "DX0", writes=[rXres[0]])
        opend = []
        for s in range(NT):
            drain_bg(99)
            drain_quant(2)
            if s + 1 < NT:
                sch.dma(G, lambda e, s=s: e.dma_start(
                    xres[(s + 1) % 2][:, :],
                    d_x[128 * (s + 1):128 * (s + 2), :]),
                    f"DX{(s + 1) % 2}", writes=[rXres[(s + 1) % 2]])
            zb = zbuf[s % 2]
            rz = rZ[s % 2]
            rx = rXres[s % 2]
            for dd in range(2):
                pb = (2 * s + dd) % 4
                for k in range(ND):
                    first = (k == 0)
                    sch.op(T, lambda e, pb=pb, k=k, s=s, dd=dd, f=first:
                           e.matmul(
                               PB[pb][:, :],
                               XT[:, 2048 * k + 128 * s:2048 * k + 128 * s + 128],
                               WBUF[:, 1024 * k + 512 * dd:
                                    1024 * k + 512 * dd + 512],
                               start=f, stop=False),
                           reads=[rWBs[k], rAOT[k][s]],
                           writes=([rPB[pb]] if first else []))
                sch.op(T, lambda e, pb=pb, dd=dd: e.matmul(
                    PB[pb][:, :], inv128[:, :],
                    bob[:, 512 * dd:512 * dd + 512], start=False, stop=True),
                    reads=[rSmall, rLate], writes=[rPB[pb]])
                vop(lambda e, pb=pb, dd=dd, zb=zb, xr=xres[s % 2]:
                    e.scalar_tensor_tensor(
                        zb[:, 512 * dd:512 * dd + 512],
                        xr[:, 512 * dd:512 * dd + 512], 1.0, PB[pb][:, :],
                        ALU.mult, ALU.add),
                    reads=[rx, rPB[pb]], writes=[rz])
            if opend:
                opend.pop(0)()
            par = s % 2
            rB = rBnP[par]
            vop(lambda e, zb=zb, p=par: e.bn_stats(
                bnst[:, 12 * p:12 * p + 6], zb[:, 0:512]),
                reads=[rz], writes=[rB])
            vop(lambda e, zb=zb, p=par: e.bn_stats(
                bnst[:, 12 * p + 6:12 * p + 12], zb[:, 512:1024]),
                reads=[rz], writes=[rB])
            vop(lambda e, p=par: e.bn_aggr(bnagg[:, 2 * p:2 * p + 2],
                                           bnst[:, 12 * p:12 * p + 12]),
                reads=[], writes=[rB])
            sop(lambda e, p=par: e.activation(
                sdcol[:, p:p + 1], bnagg[:, 2 * p + 1:2 * p + 2], AF.Sqrt,
                bias=epsc[:, 0:1]),
                reads=[rSmall], writes=[rB])
            vop(lambda e, p=par: e.reciprocal(rlncol[:, p:p + 1],
                                              sdcol[:, p:p + 1]),
                reads=[], writes=[rB])
            vop(lambda e, zb=zb, p=par: e.tensor_scalar(
                zb[:, :], zb[:, :], bnagg[:, 2 * p:2 * p + 1],
                rlncol[:, p:p + 1],
                ALU.subtract, ALU.mult), reads=[rB], writes=[rz])
            vop(lambda e, zb=zb, s=s: e.tensor_copy(
                KT[:, 1024 * s:1024 * s + 1024], zb[:, :]),
                reads=[rz], writes=[rX1N[s]])

            def emit_x1t(s=s, zb=zb, rz=rz):
                for g in range(2):
                    for q in range(4):
                        k = 4 * g + q
                        sch.op(T, lambda e, q=q, k=k, zb=zb: e.transpose(
                            PB[7][:, 128 * q:128 * q + 128],
                            zb[:, 128 * k:128 * k + 128], identf[:, :]),
                            reads=[rz], writes=[rPB[7]])
                    for q in range(4):
                        k = 4 * g + q
                        sop(lambda e, q=q, k=k, s=s: e.activation(
                            XT[:, 2048 * k + 128 * s:2048 * k + 128 * s + 128],
                            PB[7][:, 128 * q:128 * q + 128], AF.Identity,
                            scale=0.5),
                            reads=[rPB[7]], writes=[rAOT[k][s]])
            opend.append(emit_x1t)

        while opend:
            opend.pop(0)()
        wbuf_cm.__exit__(None, None, None)
        depWB = []
        for rr_ in rWBs:
            depWB += rr_.tokens()
        outst = [CM.enter_context(
            nc.sbuf_tensor(f"outst_{i}", [128, DM], F32, side="right"))
            for i in range(2)]
        hbuf = [CM.enter_context(
            nc.sbuf_tensor(f"hbuf_{i}", [128, 256], BF16, side="right"))
            for i in range(8)]
        rOut = [Reg(depWB), Reg(depWB)]
        rHb = [Reg(depWB) for _ in range(8)]

        OUT2 = [(PB[0][:, :], rPB[0]), (PB[1][:, :], rPB[1]),
                (PB[2][:, :], rPB[2]), (PB[3][:, :], rPB[3])]

        def emit_ffn2(blk, j):
            hb = hbuf[j % 8]
            rhb = rHb[j % 8]
            for sub in range(2):
                for dd in range(2):
                    po = 2 * sub + dd
                    oap, oreg = OUT2[po]
                    first = (j == 0)
                    sch.op(T, lambda e, oap=oap, sub=sub, dd=dd, j=j, hb=hb,
                           f=first: e.matmul(
                               oap,
                               hb[:, 128 * sub:128 * sub + 128],
                               W2F8[:, 1024 * j + 512 * dd:
                                    1024 * j + 512 * dd + 512],
                               start=f, stop=False),
                           reads=[rhb, rW2[j]],
                           writes=([oreg] if first else []))
                    if j == NF - 1:
                        sch.op(T, lambda e, oap=oap, dd=dd: e.matmul(
                            oap, inv128[:, :],
                            b2b[:, 512 * dd:512 * dd + 512],
                            start=False, stop=True),
                            reads=[rSmall, rLate], writes=[oreg])

        for blk in range(8):
            for j in range(NF):
                if blk == 0:
                    drain_quant(1)
                ph = 4 + (j % 2)
                rhb = rHb[j % 8]
                hb = hbuf[j % 8]
                for k in range(ND):
                    first, last = (k == 0), (k == ND - 1)
                    sch.op(T, lambda e, ph=ph, k=k, j=j, blk=blk, f=first,
                           l=last: e.matmul(
                               PB[ph][:, 0:256],
                               W1F8[:, 4096 * k + 128 * j:
                                    4096 * k + 128 * j + 128],
                               XT[:, 2048 * k + 256 * blk:
                                  2048 * k + 256 * blk + 256],
                               start=f, stop=l),
                           reads=[rW1[k][j // 8], rAOT[k][2 * blk],
                                  rAOT[k][2 * blk + 1]],
                           writes=([rPB[ph]] if (first or last) else []))
                sop(lambda e, ph=ph, j=j, hb=hb: e.activation(
                    hb[:, :], PB[ph][:, 0:256], AF.Relu,
                    bias=b1cols[:, j:j + 1], scale=1.0),
                    reads=[rPB[ph], rSmall], writes=[rhb])
                if j >= 5:
                    emit_ffn2(blk, j - 5)
            for jj in range(NF - 5, NF):
                emit_ffn2(blk, jj)
            for sub in range(2):
                s = 2 * blk + sub
                zb = zbuf[sub]
                rz = rZ[sub]
                for dd in range(2):
                    po = 2 * sub + dd
                    oap, oreg = OUT2[po]
                    vop(lambda e, oap=oap, dd=dd, zb=zb, s=s:
                        e.scalar_tensor_tensor(
                            zb[:, 512 * dd:512 * dd + 512],
                            KT[:, 1024 * s + 512 * dd:
                               1024 * s + 512 * dd + 512],
                            1.0, oap, ALU.mult, ALU.add),
                        reads=[rX1N[s], oreg], writes=[rz])
                par = sub
                rB = rBnP[par]
                vop(lambda e, zb=zb, p=par: e.bn_stats(
                    bnst[:, 12 * p:12 * p + 6], zb[:, 0:512]),
                    reads=[rz], writes=[rB])
                vop(lambda e, zb=zb, p=par: e.bn_stats(
                    bnst[:, 12 * p + 6:12 * p + 12], zb[:, 512:1024]),
                    reads=[rz], writes=[rB])
                vop(lambda e, p=par: e.bn_aggr(bnagg[:, 2 * p:2 * p + 2],
                                               bnst[:, 12 * p:12 * p + 12]),
                    reads=[], writes=[rB])
                sop(lambda e, p=par: e.activation(
                    sdcol[:, p:p + 1], bnagg[:, 2 * p + 1:2 * p + 2], AF.Sqrt,
                    bias=epsc[:, 0:1]),
                    reads=[rSmall], writes=[rB])
                vop(lambda e, p=par: e.reciprocal(rlncol[:, p:p + 1],
                                                  sdcol[:, p:p + 1]),
                    reads=[], writes=[rB])
                vop(lambda e, zb=zb, o=outst[sub], p=par: e.tensor_scalar(
                    o[:, :], zb[:, :], bnagg[:, 2 * p:2 * p + 1],
                    rlncol[:, p:p + 1],
                    ALU.subtract, ALU.mult),
                    reads=[rBnP[par], rz], writes=[rOut[sub]])
                sch.dma("sync", lambda e, s=s, o=outst[sub]: e.dma_start(
                    d_out[128 * s:128 * s + 128, :], o[:, :]),
                    f"DO{sub}", reads=[rOut[sub]])

        waited = {s_: set() for s_ in sch.n}
        for eng_name in ENGS:
            for waits, fn, sem, amt, val in sch.ops[eng_name]:
                for s_, v in waits:
                    waited[s_].add(v)
        for s_ in sch.n:
            if sch.n[s_] > 0:
                waited[s_].add(sch.n[s_])
                if sch.is_dma_sem(s_):
                    for v_ in range(16, sch.n[s_] + 1, 16):
                        waited[s_].add(v_)
        remap = {}
        newtot = {}
        for s_, vals in waited.items():
            amt_ = 16 if sch.is_dma_sem(s_) else 1
            for rank, v in enumerate(sorted(vals)):
                remap[(s_, v)] = (rank + 1) * amt_
            newtot[s_] = len(vals) * amt_

        with nc.Block() as block:
            def mk(eng_name):
                def body(e):
                    for waits, fn, sem, amt, val in sch.ops[eng_name]:
                        for s_, v in waits:
                            e.wait_ge(sems[s_], remap[(s_, v)])
                        inst = fn(e)
                        if inst is not None and val in waited[sem]:
                            inst.then_inc(sems[sem], amt)
                    if eng_name == "sync":
                        for s_ in sorted(sch.n):
                            if sch.is_dma_sem(s_) and sch.n[s_] > 0:
                                e.wait_ge(sems[s_], newtot[s_])
                return body

            block.tensor(mk("tensor"))
            block.scalar(mk("scalar"))
            block.vector(mk("vector"))
            block.gpsimd(mk("gpsimd"))
            block.sync(mk("sync"))

    return nc


_CACHE = {}


def _consts():
    ident = np.eye(128, dtype=np.float32)
    identu = np.zeros((128, 129), dtype=ml_dtypes.bfloat16)
    for p in range(128):
        identu[p, p + 1] = 1.0
    elast = np.zeros((128, 1), dtype=np.float32)
    elast[127, 0] = 1.0
    idd = np.zeros((128, 129), dtype=ml_dtypes.bfloat16)
    m0 = np.zeros((128, 129), dtype=ml_dtypes.bfloat16)
    for p in range(128):
        idd[p, p] = 1.0
        m0[p, p + 2:] = 1.0
    oxl = np.ones((128, 1), dtype=ml_dtypes.bfloat16)
    oxl[127, 0] = 0.0
    return dict(
        identf=ident, idd129=idd, identu129=identu, m0c129=m0, onesxl=oxl,
        elast=elast,
        onesrow_f=np.ones((1, 128), dtype=np.float32),
        onesrow_b=np.ones((1, 128), dtype=ml_dtypes.bfloat16),
        onesall=np.ones((128, 128), dtype=ml_dtypes.bfloat16),
        inv128=np.full((128, 128), 1.0 / 128, dtype=ml_dtypes.bfloat16),
    )


def kernel(**inputs):
    x = np.ascontiguousarray(np.asarray(inputs["L0_attn_x"], dtype=np.float32))
    wq = np.ascontiguousarray(np.asarray(inputs["L0_attn_W_q"], dtype=np.float32))
    wk = np.ascontiguousarray(np.asarray(inputs["L0_attn_W_k"], dtype=np.float32))
    wv = np.ascontiguousarray(np.asarray(inputs["L0_attn_W_v"], dtype=np.float32))
    wo = np.ascontiguousarray(np.asarray(inputs["L0_attn_W_o"], dtype=np.float32))
    bq = np.ascontiguousarray(np.asarray(inputs["L0_attn_b_q"],
                                         dtype=np.float32).reshape(8, 128))
    bk = np.ascontiguousarray(np.asarray(inputs["L0_attn_b_k"],
                                         dtype=np.float32).reshape(8, 128))
    bv = np.ascontiguousarray(np.asarray(inputs["L0_attn_b_v"],
                                         dtype=np.float32).reshape(1, 1024))
    bo = np.ascontiguousarray(np.asarray(inputs["L0_attn_b_o"],
                                         dtype=np.float32).reshape(1, 1024))
    rk = np.ascontiguousarray(np.asarray(inputs["L0_attn_rel_k_table"],
                                         dtype=np.float32))
    rv = np.ascontiguousarray(np.asarray(inputs["L0_attn_rel_v_table"],
                                         dtype=np.float32))
    w1t = np.ascontiguousarray(np.asarray(inputs["L0_ffn_W1"],
                                          dtype=np.float32).T)
    b1 = np.ascontiguousarray(np.asarray(inputs["L0_ffn_b1"],
                                         dtype=np.float32).reshape(32, 128))
    w2t = np.ascontiguousarray(np.asarray(inputs["L0_ffn_W2"],
                                          dtype=np.float32).T)
    b2 = np.ascontiguousarray(np.asarray(inputs["L0_ffn_b2"],
                                         dtype=np.float32).reshape(1, 1024))

    if "nc" not in _CACHE:
        _CACHE["nc"] = build_program()
    nc = _CACHE["nc"]
    consts = _consts()
    in_maps = []
    for b in range(8):
        m = dict(
            x_nat=x[b], xT=np.ascontiguousarray(x[b].T),
            Wq=wq, Wk=wk, Wv=wv, Wo=wo, W1T=w1t, W2T=w2t,
            bq8=bq, bk8=bk, b1_32=b1, bv_row=bv, bo_row=bo, b2_row=b2,
            relk=rk, relv=rv,
        )
        m.update(consts)
        in_maps.append(m)
    res = run_bass_kernel_spmd(nc, in_maps, core_ids=list(range(8)))
    return np.stack([np.asarray(res.results[b]["out"]) for b in range(8)], axis=0)
